# version 20
# baseline (speedup 1.0000x reference)
import numpy as np
from contextlib import ExitStack

import concourse.bass as bass
import concourse.mybir as mybir
from concourse.bass_utils import run_bass_kernel_spmd
from concourse.alu_op_type import AluOpType as A

F32 = mybir.dt.float32
BF16 = mybir.dt.bfloat16
AF = mybir.ActivationFunctionType

D = 2048
S = 2048
NH, NKV, HD = 16, 4, 128
N = 8192
K_TOP = 819
EPS = 1e-5
NEG = -1.0e30
TOK = 1024
NDC = 16
NSL = 8
NNB = 16
BITERS = 11
CHUNKS = [[0, 2, 4, 6, 9, 11, 13, 15], [1, 3, 5, 7, 8, 10, 12, 14]]
ENG = ("sync", "vector", "scalar", "tensor", "gpsimd")
OWN_SEM = {"vector": "V", "scalar": "Sc", "tensor": "T"}


class Buf:
    __slots__ = ("w", "rs", "dsem")

    def __init__(self):
        self.w = None
        self.rs = []
        self.dsem = None


class Plan:
    def __init__(self, nc):
        self.nc = nc
        self.items = {e: [] for e in ENG}
        self.sem_names = []
        self.cnt = {}
        self.marks = {e: {} for e in ENG}

    def sem(self, name):
        self.sem_names.append(name)
        self.cnt[name] = 0

    def dma_sem(self, b):
        if not hasattr(b, "dsem") or b.dsem is None:
            b.dsem = f"d{len(self.sem_names)}"
            self.sem(b.dsem)
        return b.dsem

    def emit(self, eng, fn, ins=(), outs=(), sem=None, n=1, extra=(), quiet=False):
        is_load = n == 16 and not ins
        reads = []
        for b in ins:
            if b.w is not None:
                reads.append(b.w)
        for b in outs:
            if b.w is not None and not (is_load and b.w[0] == sem and not b.rs):
                reads.append(b.w)
            reads.extend(b.rs)
        reads.extend(t for t in extra if t is not None)
        waits = []
        own = OWN_SEM.get(eng)
        for (s, v) in reads:
            if eng == "tensor" and s == own:
                continue
            if v > 0 and self.marks[eng].get(s, 0) < v:
                waits.append((s, v))
                self.marks[eng][s] = v
        if quiet:
            tok = (sem, self.cnt[sem] + n)
            self.items[eng].append((waits, fn, None, n, None))
        else:
            self.cnt[sem] += n
            tok = (sem, self.cnt[sem])
            self.items[eng].append((waits, fn, sem, n, self.cnt[sem]))
        for b in ins:
            b.rs.append(tok)
        for b in outs:
            b.w = tok
            b.rs = []
        return tok

    def barrier(self):
        toks = [(s, self.cnt[s]) for s in self.sem_names]
        for e in ENG:
            for (s, v) in toks:
                if v > 0 and self.marks[e].get(s, 0) < v:
                    self.items[e].append(([(s, v)], None, None, 1, None))
                    self.marks[e][s] = v

    def build(self):
        nc = self.nc
        with ExitStack() as ses:
            sems = {s: ses.enter_context(nc.semaphore(s)) for s in self.sem_names}
            with nc.Block() as block:
                def mk(eng_name):
                    def run(eng):
                        chk = {}
                        for (waits, fn, sem, n, pv) in self.items[eng_name]:
                            for (s, v) in waits:
                                eng.wait_ge(sems[s], v)
                            if fn is None:
                                continue
                            instr = fn(eng)
                            if sem is None:
                                continue
                            instr.then_inc(sems[sem], n)
                            chk[sem] = chk.get(sem, 0) + n
                            assert chk[sem] == pv or pv is None, (eng_name, sem, chk[sem], pv)
                    return run
                for e in ENG:
                    if self.items[e]:
                        getattr(block, e)(mk(e))


def build_program(stop_after=None):
    nc = bass.Bass("TRN2", target_bir_lowering=False)
    p = Plan(nc)
    for s in ("V", "Sc", "T"):
        p.sem(s)

    dr = nc.dram_tensor
    xTs = dr("xTs", [D, S], F32, kind="ExternalInput").ap()
    xTo = dr("xTo", [D, TOK], F32, kind="ExternalInput").ap()
    wqT = dr("wqT", [D, NH * HD], F32, kind="ExternalInput").ap()
    wkT = dr("wkT", [D, NKV * HD], F32, kind="ExternalInput").ap()
    wvT = dr("wvT", [D, NKV * HD], F32, kind="ExternalInput").ap()
    woT = dr("woT", [NH * HD, D], F32, kind="ExternalInput").ap()
    wsynT_h = dr("wsynT", [D, N], F32, kind="ExternalInput").ap()
    woutT_h = dr("woutT", [N, D], F32, kind="ExternalInput").ap()
    bo_h = dr("bo", [NDC, 128], F32, kind="ExternalInput").ap()
    bout_h = dr("bout", [NDC, 128], F32, kind="ExternalInput").ap()
    gam_h = dr("gam", [NDC, 128], F32, kind="ExternalInput").ap()
    bet_h = dr("bet", [NDC, 128], F32, kind="ExternalInput").ap()
    bsyn_h = dr("bsyn", [NNB, 512], F32, kind="ExternalInput").ap()
    diag01_h = dr("diag01", [128, 128], BF16, kind="ExternalInput").ap()
    cmask_h = dr("cmask", [8, 128, 256], F32, kind="ExternalInput").ap()
    x1_hbm = dr("x1scr", [D, TOK], F32, kind="Internal").ap()
    tqT_hbm = dr("tqscr", [D, N], BF16, kind="Internal").ap()
    toutT_hbm = dr("toscr", [N, D], BF16, kind="Internal").ap()
    y_hbm = dr("yscr", [D, TOK], F32, kind="Internal").ap()
    y_out = dr("y", [D, TOK], F32, kind="ExternalOutput").ap()
    b_x1h, b_yh, b_yo, b_tqh = Buf(), Buf(), Buf(), Buf()

    stP = ExitStack()

    sides = {}

    def sbp(es, name, shape, dt):
        side = sides.get(id(es), "left")
        return es.enter_context(nc.sbuf_tensor(name, shape, dt, side=side)), Buf()

    ones_bf, b_ones = sbp(stP, "ones_f32", [128, 128], F32)
    ones1_bf, b_ones1 = sbp(stP, "ones1_bf", [1, 512], BF16)
    diag01, b_diag01 = sbp(stP, "diag01_sb", [128, 128], BF16)
    bo_sb, b_bo = sbp(stP, "bo_sb", [128, NDC], F32)
    bout_sb, b_bout = sbp(stP, "bout_sb", [128, NDC], F32)
    gam_sb, b_gam = sbp(stP, "gam_sb", [128, NDC], F32)
    bet_sb, b_bet = sbp(stP, "bet_sb", [128, NDC], F32)
    bsyn_sl, b_bsyn = sbp(stP, "bsyn_sl", [1, 512], BF16)
    iota8, b_iota = sbp(stP, "iota8", [128, 8], F32)
    sclv, b_sclv = sbp(stP, "sclv", [128, 8], F32)
    sprt, b_sprt = sbp(stP, "sprt", [128, 128], F32)
    stt, b_stt = sbp(stP, "stt", [128, 12 * NSL], F32)
    m8t, b_m8 = sbp(stP, "m8t", [128, 8 * NSL], F32)
    musum, b_mus = sbp(stP, "musum", [128, NSL * NNB], F32)
    small, b_small = sbp(stP, "small", [128, 32], F32)
    epsc, b_eps = sbp(stP, "epsc", [128, 2], F32)

    psS = ExitStack()
    PS = [psS.enter_context(nc.psum_tensor(f"ps{i}", [128, 512], F32)) for i in range(6)]
    b_PS = [Buf() for _ in range(6)]
    PTb = [psS.enter_context(nc.psum_tensor(f"pt{i}", [128, 512], F32)) for i in range(2)]
    b_PT = [Buf() for _ in range(2)]

    def dma(q, dst, src, ins=(), outs=(), sem=None, extra=()):
        sm = p.dma_sem(outs[0])
        return p.emit(q, lambda e, d=dst, s=src: e.dma_start(d, s), ins=ins, outs=outs,
                      sem=sm, n=16, extra=extra)

    def V(fn, ins=(), outs=(), extra=()):
        return p.emit("vector", fn, ins=ins, outs=outs, sem="V", extra=extra)

    def Sc(fn, ins=(), outs=(), extra=()):
        return p.emit("scalar", fn, ins=ins, outs=outs, sem="Sc", extra=extra)

    def T(fn, ins=(), outs=(), extra=()):
        return p.emit("tensor", fn, ins=ins, outs=outs, sem="T", extra=extra)

    def Tq(fn, ins=(), outs=(), extra=()):
        return p.emit("tensor", fn, ins=ins, outs=outs, sem="T", extra=extra, quiet=True)

    dma("sync", diag01[:], diag01_h, outs=[b_diag01])
    for ci in range(NDC):
        dma("sync", bo_sb[:, ci:ci + 1], bo_h[ci:ci + 1, :].rearrange("a b -> b a"), outs=[b_bo])
        dma("sync", bout_sb[:, ci:ci + 1], bout_h[ci:ci + 1, :].rearrange("a b -> b a"), outs=[b_bout])
        dma("sync", gam_sb[:, ci:ci + 1], gam_h[ci:ci + 1, :].rearrange("a b -> b a"), outs=[b_gam])
        dma("sync", bet_sb[:, ci:ci + 1], bet_h[ci:ci + 1, :].rearrange("a b -> b a"), outs=[b_bet])
    V(lambda e: e.memset(ones_bf[:], 1.0), outs=[b_ones])
    V(lambda e: e.memset(ones1_bf[:], 1.0), outs=[b_ones1])
    for i in range(8):
        V(lambda e, i=i: e.memset(iota8[:, i:i + 1], float(i)), outs=[b_iota])
    V(lambda e: e.memset(sprt[:], 0.0), outs=[b_sprt])
    V(lambda e: e.memset(epsc[:, 0:1], EPS), outs=[b_eps])
    V(lambda e: e.memset(epsc[:, 1:2], 1e-12), outs=[b_eps])

    ph1 = ExitStack()
    sides[id(ph1)] = "right"
    xcb1 = [sbp(ph1, f"xc1_{i}", [128, 512], F32) for i in range(4)]
    ln1gb = [sbp(ph1, f"ln1g{i}", [128, NDC * 512], BF16) for i in range(2)]
    sqb = [sbp(ph1, f"sq{i}", [128, 512], F32) for i in range(2)]
    mu1b = [sbp(ph1, f"mu1_{i}", [128, 512], F32) for i in range(2)]
    r1b = [sbp(ph1, f"r1_{i}", [128, 512], F32) for i in range(2)]
    va1b = [sbp(ph1, f"va1_{i}", [128, 512], F32) for i in range(2)]
    wk_bf, b_wk = sbp(ph1, "wk_bf", [128, NDC * 512], BF16)
    wv_bf, b_wv = sbp(ph1, "wv_bf", [128, NDC * 512], BF16)
    wqb = [sbp(ph1, f"wq_bf{i}", [128, NDC * 128], BF16) for i in range(2)]

    phA = ExitStack()
    kbf, b_kbf = sbp(phA, "kbf", [128, NKV * S], BF16)
    vbf, b_vbf = sbp(phA, "vbf", [128, 16 * 512], BF16)
    qbf, b_qbf = sbp(phA, "qbf", [128, NH * TOK], BF16)
    cmask, b_cm = sbp(phA, "cmask_sb", [128, 8 * 256], F32)
    wsfb = [sbp(phA, f"wsf{i}", [128, 2048], F32) for i in range(2)]
    for qt_ in range(8):
        dma("sync", cmask[:, qt_ * 256:(qt_ + 1) * 256], cmask_h[qt_, :, :], outs=[b_cm])

    for dc in range(NDC):
        dma("gpsimd", wk_bf[:, dc * 512:(dc + 1) * 512], wkT[dc * 128:(dc + 1) * 128, :],
            outs=[b_wk])
        dma("gpsimd", wv_bf[:, dc * 512:(dc + 1) * 512], wvT[dc * 128:(dc + 1) * 128, :],
            outs=[b_wv])

    scale_jobs = []
    for (src_ap, cbase) in ((wsynT_h, 0), (woutT_h, 64)):
        nr, ncc = src_ap.shape[0] // 128, src_ap.shape[1] // 2048
        for r in range(nr):
            for c in range(ncc):
                def job(r=r, c=c, src_ap=src_ap, cbase=cbase, idx=r * ncc + c):
                    wb, bb = wsfb[idx % 2]
                    dma("sync", wb[:, 0:2048], src_ap[r * 128:(r + 1) * 128, c * 2048:(c + 1) * 2048],
                        outs=[bb])
                    Sc(lambda e, wb=wb, col=cbase + idx: e.activation(
                        wb[:, 0:2048], wb[:, 0:2048], AF.Abs, accum_out=sprt[:, col:col + 1]),
                       ins=[bb], outs=[bb, b_sprt])
                scale_jobs.append(job)
    scale_jobs.reverse()

    def drain_scale_jobs(k):
        for _ in range(min(k, len(scale_jobs))):
            scale_jobs.pop()()

    def finish_scale(base, tcol):
        cb = 0 if base == 0 else 64
        V(lambda e, cb=cb: e.tensor_scalar(
            small[:, 0:1], sprt[:, cb:cb + 64].rearrange("p a -> p a") if False else small[:, 0:1],
            1.0, None, op0=A.mult), ins=[], outs=[]) if False else None
        V(lambda e, cb=cb: e.tensor_scalar(
            sprt[:, cb:cb + 64], sprt[:, cb:cb + 64], 1.0, None, op0=A.mult, op1=A.add,
            accum_out=small[:, 0:1]), ins=[b_sprt], outs=[b_sprt, b_small])
        V(lambda e, cb=cb: e.tensor_scalar(
            sprt[:, cb:cb + 1], small[:, 0:1], 1.0, None, op0=A.mult),
          ins=[b_small], outs=[b_sprt])
        T(lambda e, cb=cb: e.matmul(PS[5][:, 0:1], ones_bf[:], sprt[:, cb:cb + 1],
                             start=True, stop=True), ins=[b_sprt, b_ones], outs=[b_PS[5]])
        V(lambda e: e.tensor_scalar(sclv[:, base + 2:base + 3], PS[5][:, 0:1], 1.0 / (D * N), None,
                                    op0=A.mult), ins=[b_PS[5]], outs=[b_sclv])
        V(lambda e: e.tensor_scalar(sclv[:, base:base + 1], sclv[:, base + 2:base + 3],
                                    0.5, 0.5e-8, op0=A.mult, op1=A.add), ins=[b_sclv], outs=[b_sclv])
        V(lambda e: e.tensor_scalar(sclv[:, base + 1:base + 2], sclv[:, base:base + 1],
                                    -1.0, None, op0=A.mult), ins=[b_sclv], outs=[b_sclv])

    quant_jobs = []
    out_jobs = []
    QB = {"cnt": [0]}
    b_tqcol = [Buf() for _ in range(N // 1024)]
    b_toutc = [Buf() for _ in range(D // 1024)]

    def mk_qjob(src_ap, dst_ap, r, c, base, dst_buf):
        def qjob():
            i = QB["cnt"][0] % 2
            QB["cnt"][0] += 1
            qwf, b_qwf = QB["qwfb"][i]
            qto, b_qto = QB["qtob"][i]
            qab, b_qa = QB["qab"]
            dma("sync", qwf[:], src_ap[r * 128:(r + 1) * 128, c * 1024:(c + 1) * 1024],
                outs=[b_qwf])
            Sc(lambda e, qwf=qwf, qto=qto: e.activation(qto[:], qwf[:], AF.Sign,
                                                        bias=sclv[:, base + 1:base + 2]),
               ins=[b_qwf, b_sclv], outs=[b_qto])
            Sc(lambda e, qwf=qwf, qab=qab: e.activation(qab[:], qwf[:], AF.Sign,
                                                        bias=sclv[:, base:base + 1]),
               ins=[b_qwf, b_sclv], outs=[b_qa])
            V(lambda e, qab=qab: e.tensor_scalar(qab[:], qab[:], 0.0, None, op0=A.min),
              ins=[b_qa], outs=[b_qa])
            V(lambda e, qto=qto, qab=qab: e.scalar_tensor_tensor(qto[:], qto[:], 0.0, qab[:],
                                                                 op0=A.max, op1=A.add),
              ins=[b_qa], outs=[b_qto])
            dma("gpsimd", dst_ap[r * 128:(r + 1) * 128, c * 1024:(c + 1) * 1024], qto[:],
                ins=[b_qto], outs=[dst_buf])
        return qjob

    for qc in range(N // 1024):
        for qr in range(D // 128):
            quant_jobs.append(mk_qjob(wsynT_h, tqT_hbm, qr, qc, 0, b_tqcol[qc]))
    for qc in range(D // 1024):
        for qr in range(N // 128):
            out_jobs.append(mk_qjob(woutT_h, toutT_hbm, qr, qc, 3, b_toutc[qc]))

    def drain_quant_jobs(k):
        for _ in range(min(k, len(quant_jobs))):
            quant_jobs.pop(0)()

    def drain_quant_to(k):
        while 128 - len(quant_jobs) < k and quant_jobs:
            quant_jobs.pop(0)()

    def drain_out_jobs(k):
        for _ in range(min(k, len(out_jobs))):
            out_jobs.pop(0)()

    def ln1_group(x_src, t0, gpar):
        ln1g, b_ln1 = ln1gb[gpar]
        mu1, b_mu1 = mu1b[gpar]
        r1, b_r1 = r1b[gpar]
        va1, b_va1 = va1b[gpar]
        for dc in range(NDC):
            xc, b_xc = xcb1[dc % 2]
            dma("sync", xc[:], x_src[dc * 128:(dc + 1) * 128, t0:t0 + 512], outs=[b_xc])
            T(lambda e, xc=xc, dc=dc: e.matmul(PS[0][:], ones_bf[:], xc[:],
                                               start=(dc == 0), stop=(dc == NDC - 1)),
              ins=[b_xc, b_ones], outs=[b_PS[0]])
            sq, b_sq = sqb[dc % 2]
            Sc(lambda e, xc=xc, sq=sq: e.activation(sq[:], xc[:], AF.Square),
               ins=[b_xc], outs=[b_sq])
            T(lambda e, sq=sq, dc=dc: e.matmul(PS[1][:], ones_bf[:], sq[:],
                                               start=(dc == 0), stop=(dc == NDC - 1)),
              ins=[b_sq, b_ones], outs=[b_PS[1]])
        V(lambda e, mu1=mu1: e.tensor_scalar(mu1[:], PS[0][:], 1.0 / D, None, op0=A.mult),
          ins=[b_PS[0]], outs=[b_mu1])
        V(lambda e, va1=va1: e.tensor_scalar(va1[:], PS[1][:], 1.0 / D, None, op0=A.mult),
          ins=[b_PS[1]], outs=[b_va1])
        sq, b_sq = sqb[0]
        V(lambda e, sq=sq, mu1=mu1: e.scalar_tensor_tensor(sq[:], mu1[:], 1.0, mu1[:],
                                                           op0=A.mult, op1=A.mult),
          ins=[b_mu1], outs=[b_sq])
        V(lambda e, sq=sq, va1=va1: e.tensor_tensor(va1[:], va1[:], sq[:], op=A.subtract),
          ins=[b_sq, b_va1], outs=[b_va1])
        Sc(lambda e, r1=r1, va1=va1: e.activation(r1[:], va1[:], AF.Ln, bias=epsc[:, 0:1]),
           ins=[b_va1, b_eps], outs=[b_r1])
        Sc(lambda e, r1=r1: e.activation(r1[:], r1[:], AF.Exp, scale=-0.5), ins=[b_r1], outs=[b_r1])
        for dc in range(NDC):
            xc, b_xc = xcb1[2 + dc % 2]
            dma("sync", xc[:], x_src[dc * 128:(dc + 1) * 128, t0:t0 + 512], outs=[b_xc])
            V(lambda e, xc=xc, mu1=mu1: e.tensor_tensor(xc[:], xc[:], mu1[:], op=A.subtract),
              ins=[b_mu1], outs=[b_xc])
            V(lambda e, dc=dc, xc=xc, r1=r1, ln1g=ln1g: e.tensor_tensor(
                ln1g[:, dc * 512:(dc + 1) * 512], xc[:], r1[:], op=A.mult),
                ins=[b_xc, b_r1], outs=[b_ln1])
        return ln1g, b_ln1

    for g in range(4):
        ln1g, b_ln1 = ln1_group(xTs, g * 512, g % 2)
        for kb in range(NKV):
            for dc in range(NDC):
                (T if dc == NDC - 1 else Tq)(lambda e, kb=kb, dc=dc, ln1g=ln1g: e.matmul(
                    PS[2][:], wk_bf[:, dc * 512 + kb * 128: dc * 512 + kb * 128 + 128],
                    ln1g[:, dc * 512:(dc + 1) * 512],
                    start=(dc == 0), stop=(dc == NDC - 1)),
                    ins=[b_ln1, b_wk], outs=[b_PS[2]])
            Sc(lambda e, kb=kb, g=g: e.copy(kbf[:, kb * S + g * 512: kb * S + g * 512 + 512],
                                            PS[2][:]), ins=[b_PS[2]], outs=[b_kbf])
        for slc in range(4):
            for dc in range(NDC):
                (T if dc == NDC - 1 else Tq)(lambda e, slc=slc, dc=dc, ln1g=ln1g: e.matmul(
                    PS[3][:], ln1g[:, dc * 512 + slc * 128: dc * 512 + slc * 128 + 128],
                    wv_bf[:, dc * 512:(dc + 1) * 512],
                    start=(dc == 0), stop=(dc == NDC - 1)),
                    ins=[b_ln1, b_wv], outs=[b_PS[3]])
            Sc(lambda e, kt=g * 4 + slc: e.copy(vbf[:, kt * 512:(kt + 1) * 512], PS[3][:]),
               ins=[b_PS[3]], outs=[b_vbf])
        drain_scale_jobs(8)

    for g in range(2):
        ln1g, b_ln1 = ln1_group(xTo, g * 512, g % 2)
        for hb in range(NH):
            wq, b_wq = wqb[hb % 2]
            dma("gpsimd", wq[:].rearrange("p (c q) -> p c q", q=128),
                wqT[:, hb * 128:(hb + 1) * 128].rearrange("(c p) q -> p c q", p=128),
                outs=[b_wq])
            for dc in range(NDC):
                (T if dc == NDC - 1 else Tq)(lambda e, dc=dc, wq=wq, ln1g=ln1g: e.matmul(
                    PS[4][:], wq[:, dc * 128:(dc + 1) * 128], ln1g[:, dc * 512:(dc + 1) * 512],
                    start=(dc == 0), stop=(dc == NDC - 1)),
                    ins=[b_ln1, b_wq], outs=[b_PS[4]])
            Sc(lambda e, hb=hb, g=g: e.copy(qbf[:, hb * TOK + g * 512: hb * TOK + g * 512 + 512],
                                            PS[4][:]), ins=[b_PS[4]], outs=[b_qbf])
            drain_scale_jobs(2)
    finish_scale(0, 1)

    p.barrier()
    if stop_after == "p1":
        return nc, p
    ph1.close()

    ph2 = ExitStack()
    sides[id(ph2)] = "right"
    p_bfb = [sbp(ph2, f"p_bf{i}", [128, 2048], BF16) for i in range(2)]
    pT, b_pT = sbp(ph2, "pT", [128, 16 * TOK], BF16)
    a_bf, b_a = sbp(ph2, "a_bf", [128, NH * TOK], BF16)
    diag_rb = [sbp(ph2, f"diag_r{i}", [128, 512], BF16) for i in range(2)]
    junk_sb = [sbp(ph2, f"junk_s{i}", [128, 512], BF16) for i in range(2)]
    mxsb = []
    for ii in range(2):
        _mt, _ = sbp(ph2, f"mxs{ii}", [128, 8], F32)
        mxsb.append((_mt, [Buf() for _ in range(4)], Buf()))
    smallb = [sbp(ph2, f"small2{i}", [128, 16], F32) for i in range(2)]
    pTv = pT[:, 0:16 * TOK].rearrange("p (k t) -> p k t", t=TOK)

    for kt in range(16):
        qmin = (kt // 2) * 128
        if qmin > 0:
            V(lambda e, kt=kt, qmin=qmin: e.memset(pT[:, kt * TOK: kt * TOK + qmin], 0.0),
              outs=[b_pT])

    rrc = [0]
    its = [(h, qt) for h in range(NH) for qt in range(NSL)]

    def st1(i):
        h, qt = its[i]
        kv = h // 4
        pbuf, b_pb = p_bfb[i % 2]
        junk_s, b_js = junk_sb[i % 2]
        mxs, mxc, b_mxg = mxsb[i % 2]
        sm2, b_sm2 = smallb[i % 2]
        kmax = 256 * (qt + 1)
        nk5 = (kmax + 511) // 512
        for j in range(nk5):
            w = min(512, kmax - j * 512)
            r = rrc[0] % 4
            rrc[0] += 1
            psb = b_PS[r]
            T(lambda e, h=h, qt=qt, j=j, w=w, kv=kv, r=r: e.matmul(
                PS[r][:, 0:w], qbf[:, h * TOK + qt * 128: h * TOK + qt * 128 + 128],
                kbf[:, kv * S + j * 512: kv * S + j * 512 + w], start=True, stop=True),
                ins=[b_qbf, b_kbf], outs=[psb])
            if j == nk5 - 1:
                V(lambda e, w=w, r=r, qt=qt: e.tensor_tensor(
                    PS[r][:, w - 256:w], PS[r][:, w - 256:w],
                    cmask[:, qt * 256:(qt + 1) * 256], op=A.add), ins=[b_cm, psb], outs=[psb])
            V(lambda e, j=j, w=w, r=r, junk_s=junk_s, mxs=mxs: e.tensor_scalar(
                junk_s[:, 0:w], PS[r][:, 0:w], -1.0, None, op0=A.mult, op1=A.min,
                accum_out=mxs[:, j:j + 1]), ins=[psb], outs=[b_js, mxc[j]])
            Sc(lambda e, j=j, w=w, r=r, pbuf=pbuf, mxs=mxs, sm2=sm2: e.activation(
                pbuf[:, j * 512: j * 512 + w], PS[r][:, 0:w], AF.Exp,
                bias=mxs[:, j:j + 1], accum_out=sm2[:, j:j + 1]),
                ins=[psb, mxc[j]], outs=[b_pb, b_sm2])

    def st2(i):
        h, qt = its[i]
        pbuf, b_pb = p_bfb[i % 2]
        diag_r, b_drg = diag_rb[i % 2]
        mxs, mxc, b_mxg = mxsb[i % 2]
        sm2, b_sm2 = smallb[i % 2]
        kmax = 256 * (qt + 1)
        nk5 = (kmax + 511) // 512
        if nk5 > 1:
            V(lambda e, nk5=nk5, mxs=mxs: e.tensor_scalar(
                mxs[:, 4:4 + nk5], mxs[:, 0:nk5], 1.0, None, op0=A.mult, op1=A.min,
                accum_out=mxs[:, 7:8]), ins=mxc[0:nk5], outs=[b_mxg])
        else:
            V(lambda e, mxs=mxs: e.tensor_scalar(mxs[:, 7:8], mxs[:, 0:1], 1.0, None,
                                                 op0=A.mult), ins=mxc[0:1], outs=[b_mxg])
        Sc(lambda e, nk5=nk5, mxs=mxs: e.activation(mxs[:, 4:4 + nk5], mxs[:, 0:nk5],
                                                    AF.Exp, scale=-1.0, bias=mxs[:, 7:8]),
           ins=mxc[0:nk5] + [b_mxg], outs=[b_mxg])
        V(lambda e, nk5=nk5, sm2=sm2, mxs=mxs: e.scalar_tensor_tensor(
            sm2[:, 8:8 + nk5], sm2[:, 0:nk5], 1.0, mxs[:, 4:4 + nk5],
            op0=A.mult, op1=A.mult, accum_out=sm2[:, 12:13]),
            ins=[b_sm2, b_mxg], outs=[b_sm2])
        V(lambda e, sm2=sm2: e.reciprocal(sm2[:, 13:14], sm2[:, 12:13]),
          ins=[b_sm2], outs=[b_sm2])
        V(lambda e, nk5=nk5, sm2=sm2, mxs=mxs: e.tensor_scalar(
            sm2[:, 8:8 + nk5], mxs[:, 4:4 + nk5], sm2[:, 13:14], None,
            op0=A.mult), ins=[b_sm2, b_mxg], outs=[b_sm2])
        for j in range(nk5):
            V(lambda e, j=j, diag_r=diag_r, sm2=sm2: e.tensor_scalar(
                diag_r[:, j * 128:(j + 1) * 128], diag01[:], sm2[:, 8 + j:9 + j],
                None, op0=A.mult), ins=[b_diag01, b_sm2], outs=[b_drg])
        nkb = 2 * (qt + 1)
        for kt0 in range(0, nkb, 4):
            nk = min(4, nkb - kt0)
            pt_i = (kt0 // 4) % 2
            for k2 in range(nk):
                jblk = (kt0 + k2) // 4
                T(lambda e, k2=k2, kt0=kt0, pt_i=pt_i, pbuf=pbuf, diag_r=diag_r,
                  jblk=jblk: e.matmul(
                    PTb[pt_i][:, k2 * 128:(k2 + 1) * 128],
                    pbuf[:, (kt0 + k2) * 128:(kt0 + k2 + 1) * 128],
                    diag_r[:, jblk * 128:(jblk + 1) * 128],
                    start=True, stop=True),
                    ins=[b_pb, b_drg], outs=[b_PT[pt_i]])
            Sc(lambda e, kt0=kt0, nk=nk, qt=qt, pt_i=pt_i: e.copy(
                pTv[:, kt0:kt0 + nk, qt * 128:(qt + 1) * 128],
                PTb[pt_i][:, 0:nk * 128].rearrange("p (k t) -> p k t", t=128)),
               ins=[b_PT[pt_i]], outs=[b_pT])

    def pv(h):
        kv = h // 4
        for qh in range(2):
            for kt in range(16):
                (T if kt == 15 else Tq)(lambda e, qh=qh, kt=kt, kv=kv: e.matmul(
                    PS[4 + qh][:], vbf[:, kt * 512 + kv * 128: kt * 512 + kv * 128 + 128],
                    pT[:, kt * TOK + qh * 512: kt * TOK + qh * 512 + 512],
                    start=(kt == 0), stop=(kt == 15)),
                    ins=[b_vbf, b_pT], outs=[b_PS[4 + qh]])
            Sc(lambda e, h=h, qh=qh: e.copy(a_bf[:, h * TOK + qh * 512: h * TOK + qh * 512 + 512],
                                            PS[4 + qh][:]), ins=[b_PS[4 + qh]], outs=[b_a])

    for i in range(len(its) + 1):
        if i > 0:
            st2(i - 1)
            if its[i - 1][1] == NSL - 1:
                pv(its[i - 1][0])
                if stop_after == ("p2pv", its[i - 1][0]):
                    p.barrier()
                    return nc, p
        if i < len(its):
            st1(i)
        if i % 4 == 0:
            drain_scale_jobs(1)
    drain_scale_jobs(len(scale_jobs))

    p.barrier()
    if stop_after == "p2":
        return nc, p
    phA.close()

    ph3 = ExitStack()
    sides[id(ph3)] = "right"
    wob = [sbp(ph3, f"wo_bf{i}", [128, NDC * 128], BF16) for i in range(2)]
    attn, b_at = sbp(ph3, "attn", [128, NDC * TOK], F32)
    sq3b = [sbp(ph3, f"sq3{i}", [128, 512], F32) for i in range(2)]
    mu2, b_mu2 = sbp(ph3, "mu2", [128, TOK], F32)
    r2, b_r2 = sbp(ph3, "r2", [128, TOK], F32)
    va2, b_va2 = sbp(ph3, "va2", [128, TOK], F32)
    xcb = [sbp(ph3, f"xc{i}", [128, TOK], F32) for i in range(2)]
    QB.update(qwfb=[sbp(ph3, f"q3wf{i}", [128, 1024], F32) for i in range(2)],
              qab=sbp(ph3, "q3a", [128, 1024], BF16),
              qtob=[sbp(ph3, f"q3to{i}", [128, 1024], BF16) for i in range(2)])
    finish_scale(3, 3)

    for blk in range(NDC):
        wo, b_wo = wob[blk % 2]
        dma("gpsimd", wo[:].rearrange("p (c q) -> p c q", q=128),
            woT[:, blk * 128:(blk + 1) * 128].rearrange("(c p) q -> p c q", p=128),
            outs=[b_wo])
        for qh in range(2):
            for dc in range(NDC):
                (T if dc == NDC - 1 else Tq)(lambda e, qh=qh, dc=dc, wo=wo: e.matmul(
                    PS[qh][:], wo[:, dc * 128:(dc + 1) * 128],
                    a_bf[:, dc * TOK + qh * 512: dc * TOK + qh * 512 + 512],
                    start=(dc == 0), stop=(dc == NDC - 1)),
                    ins=[b_a, b_wo], outs=[b_PS[qh]])
            Sc(lambda e, blk=blk, qh=qh: e.activation(
                attn[:, blk * TOK + qh * 512: blk * TOK + qh * 512 + 512], PS[qh][:], AF.Identity,
                bias=bo_sb[:, blk:blk + 1]), ins=[b_PS[qh], b_bo], outs=[b_at])
            drain_quant_jobs(1)
            drain_out_jobs(1)
    for qh in range(2):
        for dc in range(NDC):
            T(lambda e, dc=dc, qh=qh: e.matmul(
                PS[qh][:], ones_bf[:], attn[:, dc * TOK + qh * 512: dc * TOK + qh * 512 + 512],
                start=(dc == 0), stop=(dc == NDC - 1)), ins=[b_at, b_ones], outs=[b_PS[qh]])
        V(lambda e, qh=qh: e.tensor_scalar(mu2[:, qh * 512:(qh + 1) * 512], PS[qh][:], 1.0 / D, None,
                                           op0=A.mult), ins=[b_PS[qh]], outs=[b_mu2])
        for dc in range(NDC):
            sq, b_sq = sq3b[dc % 2]
            Sc(lambda e, dc=dc, qh=qh, sq=sq: e.activation(
                sq[:], attn[:, dc * TOK + qh * 512: dc * TOK + qh * 512 + 512], AF.Square),
                ins=[b_at], outs=[b_sq])
            T(lambda e, sq=sq, dc=dc, qh=qh: e.matmul(PS[2 + qh][:], ones_bf[:], sq[:],
                                                      start=(dc == 0), stop=(dc == NDC - 1)),
              ins=[b_sq, b_ones], outs=[b_PS[2 + qh]])
        V(lambda e, qh=qh: e.tensor_scalar(va2[:, qh * 512:(qh + 1) * 512], PS[2 + qh][:], 1.0 / D,
                                           None, op0=A.mult), ins=[b_PS[2 + qh]], outs=[b_va2])
    V(lambda e: e.scalar_tensor_tensor(r2[:], mu2[:], 1.0, mu2[:], op0=A.mult, op1=A.mult),
      ins=[b_mu2], outs=[b_r2])
    V(lambda e: e.tensor_tensor(va2[:], va2[:], r2[:], op=A.subtract), ins=[b_r2, b_va2], outs=[b_va2])
    Sc(lambda e: e.activation(r2[:], va2[:], AF.Ln, bias=epsc[:, 0:1]), ins=[b_va2, b_eps], outs=[b_r2])
    Sc(lambda e: e.activation(r2[:], r2[:], AF.Exp, scale=-0.5), ins=[b_r2], outs=[b_r2])
    for dc in range(NDC):
        xc, b_xc = xcb[dc % 2]
        dma("sync", xc[:], xTo[dc * 128:(dc + 1) * 128, :], outs=[b_xc])
        V(lambda e, dc=dc: e.tensor_tensor(attn[:, dc * TOK:(dc + 1) * TOK],
                                           attn[:, dc * TOK:(dc + 1) * TOK], mu2[:], op=A.subtract),
          ins=[b_mu2], outs=[b_at])
        V(lambda e, dc=dc: e.tensor_tensor(attn[:, dc * TOK:(dc + 1) * TOK],
                                           attn[:, dc * TOK:(dc + 1) * TOK], r2[:], op=A.mult),
          ins=[b_r2], outs=[b_at])
        V(lambda e, dc=dc, xc=xc: e.tensor_tensor(xc[:], xc[:], attn[:, dc * TOK:(dc + 1) * TOK],
                                                  op=A.add), ins=[b_at], outs=[b_xc])
        dma("gpsimd", x1_hbm[dc * 128:(dc + 1) * 128, :], xc[:], ins=[b_xc], outs=[b_x1h])
    p.barrier()
    if stop_after == "p3":
        return nc, p
    ph3.close()
    ph2.close()

    ph4 = ExitStack()
    zar, b_zar = sbp(ph4, "zar", [128, NSL * N], BF16)
    b_zsl = [Buf() for _ in range(NSL)]
    ph4x = ExitStack()
    ln3x, b_l3 = sbp(ph4x, "ln3x", [128, NDC * TOK], BF16)
    zv = zar[:, 0:NSL * N].rearrange("p (s n) -> p s n", n=N)

    ph4s = ExitStack()
    sides[id(ph4s)] = "right"
    mu3, b_mu3 = sbp(ph4s, "mu3", [128, TOK], F32)
    r3, b_r3 = sbp(ph4s, "r3", [128, TOK], F32)
    va3, b_va3 = r3, b_r3
    sq4b = [sbp(ph4s, f"sq4{i}", [128, 512], F32) for i in range(2)]
    x4b = [sbp(ph4s, f"x4{i}", [128, TOK], F32) for i in range(2)]

    for dc in range(NDC):
        xc, b_xc = x4b[dc % 2]
        dma("sync", xc[:], x1_hbm[dc * 128:(dc + 1) * 128, :], ins=[b_x1h], outs=[b_xc])
        for qh in range(2):
            T(lambda e, dc=dc, qh=qh, xc=xc: e.matmul(
                PS[qh][:], ones_bf[:], xc[:, qh * 512:(qh + 1) * 512],
                start=(dc == 0), stop=(dc == NDC - 1)), ins=[b_xc, b_ones], outs=[b_PS[qh]])
            sq, b_sq = sq4b[qh]
            Sc(lambda e, xc=xc, qh=qh, sq=sq: e.activation(sq[:], xc[:, qh * 512:(qh + 1) * 512],
                                                           AF.Square), ins=[b_xc], outs=[b_sq])
            T(lambda e, sq=sq, dc=dc, qh=qh: e.matmul(PS[2 + qh][:], ones_bf[:], sq[:],
                                                      start=(dc == 0), stop=(dc == NDC - 1)),
              ins=[b_sq, b_ones], outs=[b_PS[2 + qh]])
    for qh in range(2):
        V(lambda e, qh=qh: e.tensor_scalar(mu3[:, qh * 512:(qh + 1) * 512], PS[qh][:], 1.0 / D, None,
                                           op0=A.mult), ins=[b_PS[qh]], outs=[b_mu3])
        V(lambda e, qh=qh: e.tensor_scalar(r3[:, qh * 512:(qh + 1) * 512], PS[2 + qh][:], 1.0 / D,
                                           None, op0=A.mult), ins=[b_PS[2 + qh]], outs=[b_r3])
    V(lambda e: e.scalar_tensor_tensor(small[:, 0:1], mu3[:, 0:1], 1.0, mu3[:, 0:1],
                                       op0=A.mult, op1=A.mult), ins=[b_mu3], outs=[b_small]) if False else None
    V(lambda e: e.scalar_tensor_tensor(r3[:], mu3[:], -1.0, mu3[:], op0=A.mult, op1=A.mult),
      ins=[b_mu3], outs=[b_r3]) if False else None
    for qh in range(2):
        sq, b_sq = sq4b[qh]
        V(lambda e, qh=qh, sq=sq: e.scalar_tensor_tensor(
            sq[:], mu3[:, qh * 512:(qh + 1) * 512], 1.0, mu3[:, qh * 512:(qh + 1) * 512],
            op0=A.mult, op1=A.mult), ins=[b_mu3], outs=[b_sq])
        V(lambda e, qh=qh, sq=sq: e.tensor_tensor(r3[:, qh * 512:(qh + 1) * 512],
                                                  r3[:, qh * 512:(qh + 1) * 512], sq[:],
                                                  op=A.subtract), ins=[b_sq], outs=[b_r3])
    Sc(lambda e: e.activation(r3[:], r3[:], AF.Ln, bias=epsc[:, 0:1]), ins=[b_r3, b_eps], outs=[b_r3])
    Sc(lambda e: e.activation(r3[:], r3[:], AF.Exp, scale=-0.5), ins=[b_r3], outs=[b_r3])
    V(lambda e: e.tensor_scalar(r3[:], r3[:], sclv[:, 2:3], None, op0=A.mult),
      ins=[b_sclv], outs=[b_r3])
    for dc in range(NDC):
        xc, b_xc = x4b[dc % 2]
        dma("sync", xc[:], x1_hbm[dc * 128:(dc + 1) * 128, :], ins=[b_x1h], outs=[b_xc])
        V(lambda e, xc=xc: e.tensor_tensor(xc[:], xc[:], mu3[:], op=A.subtract),
          ins=[b_mu3], outs=[b_xc])
        V(lambda e, dc=dc, xc=xc: e.tensor_tensor(ln3x[:, dc * TOK:(dc + 1) * TOK], xc[:], r3[:],
                                                  op=A.mult), ins=[b_xc, b_r3], outs=[b_l3])
    p.barrier()
    ph4s.close()

    ph4w = ExitStack()
    sides[id(ph4w)] = "right"
    tqhb = [sbp(ph4w, f"tqh{i}", [128, 8 * 512], BF16) for i in range(2)]
    QB.update(qwfb=[sbp(ph4w, f"q4wf{i}", [128, 1024], F32) for i in range(2)],
              qab=sbp(ph4w, "q4a", [128, 1024], BF16),
              qtob=[sbp(ph4w, f"q4to{i}", [128, 1024], BF16) for i in range(2)])
    PSL = PS[0:6] + PTb
    b_PSL = b_PS[0:6] + b_PT

    for nb in range(NNB):
        drain_quant_to((nb // 2 + 1) * 16)
        for half in range(2):
            tqh, b_tqh2 = tqhb[(nb * 2 + half) % 2]
            dma("sync", tqh[:].rearrange("p (c q) -> p c q", q=512),
                tqT_hbm[half * 1024:(half + 1) * 1024, nb * 512:(nb + 1) * 512]
                .rearrange("(c p) q -> p c q", p=128),
                ins=[b_tqcol[nb // 2]], outs=[b_tqh2])
            if half == 0:
                dma("gpsimd", bsyn_sl[:], bsyn_h[nb:nb + 1, :], outs=[b_bsyn])
            for sl in range(NSL):
                for d8 in range(8):
                    dc = half * 8 + d8
                    Tq(lambda e, sl=sl, dc=dc, d8=d8, tqh=tqh: e.matmul(
                        PSL[sl][:], ln3x[:, dc * TOK + sl * 128: dc * TOK + sl * 128 + 128],
                        tqh[:, d8 * 512:(d8 + 1) * 512], start=(dc == 0), stop=False),
                        ins=[b_l3, b_tqh2], outs=[b_PSL[sl]])
            drain_quant_jobs(4)
        for sl in range(NSL):
            T(lambda e, sl=sl: e.matmul(PSL[sl][:], ones1_bf[:, 0:128], bsyn_sl[:],
                                        start=False, stop=True), ins=[b_ones1, b_bsyn],
              outs=[b_PSL[sl]])
            Sc(lambda e, sl=sl, nb=nb: e.activation(
                zar[:, sl * N + nb * 512: sl * N + nb * 512 + 512], PSL[sl][:], AF.Identity,
                accum_out=musum[:, sl * NNB + nb: sl * NNB + nb + 1]),
                ins=[b_PSL[sl]], outs=[b_zsl[sl], b_mus])
    drain_quant_jobs(len(quant_jobs))
    p.barrier()
    if stop_after == "p4z":
        return nc, p
    ph4w.close()
    ph4x.close()

    ph4t = ExitStack()
    sides[id(ph4t)] = "right"
    junkz, b_jz = sbp(ph4t, "junkz", [128, N], BF16)
    junkz2, b_jz2 = sbp(ph4t, "junkz2", [128, N], BF16)
    QB.update(qwfb=[sbp(ph4t, f"q5wf{i}", [128, 1024], F32) for i in range(2)],
              qab=sbp(ph4t, "q5a", [128, 1024], BF16),
              qtob=[sbp(ph4t, f"q5to{i}", [128, 1024], BF16) for i in range(2)])
    LO, WID, CHI, TT, CC, GE, GN, DDc, R1, TS, MU, SD = range(12)
    b_cc = [Buf() for _ in range(NSL)]

    def c_(v, sl):
        return stt[:, v * NSL + sl: v * NSL + sl + 1]

    def call(v):
        return stt[:, v * NSL:(v + 1) * NSL]

    for sl in range(NSL):
        V(lambda e, sl=sl: e.tensor_scalar(
            small[:, 0:NNB], musum[:, sl * NNB:(sl + 1) * NNB], 1.0, None, op0=A.mult, op1=A.add,
            accum_out=c_(MU, sl)), ins=[b_mus], outs=[b_small, b_stt])
        Sc(lambda e, sl=sl: e.activation(junkz[:], zar[:, sl * N:(sl + 1) * N], AF.Square,
                                         accum_out=c_(SD, sl)),
           ins=[b_zsl[sl]], outs=[b_jz, b_stt])
    V(lambda e: e.tensor_scalar(call(MU), call(MU), 1.0 / N, None, op0=A.mult),
      ins=[b_stt], outs=[b_stt])
    V(lambda e: e.tensor_scalar(call(SD), call(SD), 1.0 / N, None, op0=A.mult),
      ins=[b_stt], outs=[b_stt])
    V(lambda e: e.scalar_tensor_tensor(call(DDc), call(MU), 1.0, call(MU), op0=A.mult, op1=A.mult),
      ins=[b_stt], outs=[b_stt])
    V(lambda e: e.tensor_tensor(call(SD), call(SD), call(DDc), op=A.subtract),
      ins=[b_stt], outs=[b_stt])
    Sc(lambda e: e.activation(call(SD), call(SD), AF.Ln, bias=epsc[:, 1:2]), ins=[b_stt, b_eps], outs=[b_stt])
    Sc(lambda e: e.activation(call(SD), call(SD), AF.Exp, scale=0.5), ins=[b_stt], outs=[b_stt])
    V(lambda e: e.tensor_scalar(call(WID), call(SD), 0.18, None, op0=A.mult), ins=[b_stt], outs=[b_stt])
    V(lambda e: e.scalar_tensor_tensor(call(LO), call(SD), 1.1516, call(MU), op0=A.mult, op1=A.add),
      ins=[b_stt], outs=[b_stt])
    V(lambda e: e.memset(call(CHI), 0.0), outs=[b_stt])

    def gc(v, g, o=0, w=4):
        return stt[:, v * NSL + 4 * g + o: v * NSL + 4 * g + o + w]

    def biter(g):
        V(lambda e, g=g: e.tensor_scalar(gc(WID, g), gc(WID, g), 0.5, None, op0=A.mult),
          ins=[b_stt], outs=[b_stt])
        V(lambda e, g=g: e.tensor_tensor(gc(TT, g), gc(LO, g), gc(WID, g), op=A.add),
          ins=[b_stt], outs=[b_stt])
        so, sw = (2, 2) if g == 0 else (1, 3)
        V(lambda e, g=g, so=so, sw=sw: e.tensor_scalar(gc(MU, g, so, sw), gc(TT, g, so, sw),
                                                       -1.0, None,
                                         op0=A.mult), ins=[b_stt], outs=[b_stt])
        for sl in ((0, 1) if g == 0 else (4,)):
            V(lambda e, sl=sl: e.tensor_scalar(
                junkz[:], zar[:, sl * N:(sl + 1) * N], c_(TT, sl), None, op0=A.is_ge, op1=A.add,
                accum_out=c_(CC, sl)), ins=[b_zsl[sl], b_stt], outs=[b_jz, b_cc[sl]])
        for sl in ((2, 3) if g == 0 else (5, 6, 7)):
            Sc(lambda e, sl=sl: e.activation(
                junkz2[:], zar[:, sl * N:(sl + 1) * N], AF.Sign, bias=c_(MU, sl),
                accum_out=c_(CC, sl)), ins=[b_zsl[sl], b_stt], outs=[b_jz2, b_cc[sl]])
        V(lambda e, g=g, so=so, sw=sw: e.tensor_scalar(
            gc(CC, g, so, sw), gc(CC, g, so, sw), 0.5, float(N) / 2, op0=A.mult, op1=A.add),
            ins=[b_cc[4 * g + so + i] for i in range(sw)],
            outs=[b_cc[4 * g + so + i] for i in range(sw)])
        V(lambda e, g=g: e.tensor_scalar(gc(GE, g), gc(CC, g), float(K_TOP), None, op0=A.is_ge),
          ins=[b_cc[4 * g + i] for i in range(4)], outs=[b_stt])
        V(lambda e, g=g: e.tensor_scalar(gc(GN, g), gc(GE, g), -1.0, 1.0, op0=A.mult, op1=A.add),
          ins=[b_stt], outs=[b_stt])
        V(lambda e, g=g: e.tensor_tensor(gc(DDc, g), gc(CC, g), gc(CHI, g), op=A.subtract),
          ins=[b_cc[4 * g + i] for i in range(4)] + [b_stt], outs=[b_stt])
        for sl in range(4 * g, 4 * g + 4):
            V(lambda e, sl=sl: e.scalar_tensor_tensor(
                c_(CHI, sl), c_(DDc, sl), c_(GN, sl), c_(CHI, sl), op0=A.mult, op1=A.add),
                ins=[b_stt], outs=[b_stt])
            V(lambda e, sl=sl: e.scalar_tensor_tensor(
                c_(LO, sl), c_(GE, sl), c_(WID, sl), c_(LO, sl), op0=A.mult, op1=A.add),
                ins=[b_stt], outs=[b_stt])

    def gtail(g):
        V(lambda e, g=g: e.tensor_tensor(gc(TT, g), gc(LO, g), gc(WID, g), op=A.add),
          ins=[b_stt], outs=[b_stt])
        V(lambda e, g=g: e.tensor_scalar(gc(R1, g), gc(CHI, g), -float(K_TOP - 1), -1.0,
                                         op0=A.add, op1=A.mult), ins=[b_stt], outs=[b_stt])
        V(lambda e, g=g: e.tensor_scalar(gc(R1, g), gc(R1, g), 7.0, 0.0, op0=A.min, op1=A.max),
          ins=[b_stt], outs=[b_stt])

    def fin(sl):
        jz, b_jzx = (junkz, b_jz) if sl % 2 == 0 else (junkz2, b_jz2)
        V(lambda e, sl=sl: e.scalar_tensor_tensor(
            jz[:], zar[:, sl * N:(sl + 1) * N], c_(TT, sl), zar[:, sl * N:(sl + 1) * N], op0=A.is_lt, op1=A.mult),
            ins=[b_zsl[sl], b_stt], outs=[b_jzx])
        V(lambda e, sl=sl: e.max(out=m8t[:, sl * 8:(sl + 1) * 8], in_=jz[:]),
          ins=[b_jzx], outs=[b_m8])
        V(lambda e, sl=sl: e.scalar_tensor_tensor(
            small[:, 0:8], iota8[:], c_(R1, sl), m8t[:, sl * 8:(sl + 1) * 8],
            op0=A.is_equal, op1=A.mult, accum_out=c_(TS, sl)),
            ins=[b_iota, b_m8, b_stt], outs=[b_small, b_stt])
        V(lambda e, sl=sl: e.tensor_scalar(
            jz[:], zar[:, sl * N:(sl + 1) * N], c_(TS, sl), None, op0=A.is_ge),
            ins=[b_zsl[sl], b_stt], outs=[b_jzx])
        V(lambda e, sl=sl: e.tensor_scalar(
            zar[:, sl * N:(sl + 1) * N], zar[:, sl * N:(sl + 1) * N], 0.0, None, op0=A.max), ins=[b_zsl[sl]], outs=[b_zsl[sl]])
        V(lambda e, sl=sl: e.tensor_tensor(
            zar[:, sl * N:(sl + 1) * N], zar[:, sl * N:(sl + 1) * N], jz[:], op=A.mult), ins=[b_jzx, b_zsl[sl]],
            outs=[b_zsl[sl]])
        for nc4 in range(0, 64, 4):
            pt_i = (nc4 // 4) % 2
            for k2 in range(4):
                T(lambda e, sl=sl, nc4=nc4, k2=k2, pt_i=pt_i: e.matmul(
                    PTb[pt_i][:, k2 * 128:(k2 + 1) * 128],
                    zar[:, sl * N + (nc4 + k2) * 128: sl * N + (nc4 + k2 + 1) * 128],
                    diag01[:], start=True, stop=True),
                    ins=[b_zsl[sl], b_diag01], outs=[b_PT[pt_i]])
            Sc(lambda e, nc4=nc4, pt_i=pt_i: e.copy(jz[:, nc4 * 128:(nc4 + 4) * 128],
                                                    PTb[pt_i][:]),
               ins=[b_PT[pt_i]], outs=[b_jzx])
        V(lambda e, sl=sl: e.tensor_copy(zar[:, sl * N:(sl + 1) * N], jz[:]), ins=[b_jzx], outs=[b_zsl[sl]])

    for it in range(BITERS):
        biter(0)
        biter(1)
        drain_out_jobs(3)
    gtail(0)
    gtail(1)
    for sl in range(NSL):
        fin(sl)
    p.barrier()
    if stop_after == "p4t":
        return nc, p
    ph4t.close()

    ph5 = ExitStack()
    sides[id(ph5)] = "right"
    toutb = [sbp(ph5, f"tout{i}", [128, 32 * 128], BF16) for i in range(2)]
    ybf = [sbp(ph5, f"yb{i}", [128, 512], F32) for i in range(2)]
    QB.update(qwfb=[sbp(ph5, f"q6wf{i}", [128, 1024], F32) for i in range(2)],
              qab=sbp(ph5, "q6a", [128, 1024], BF16),
              qtob=[sbp(ph5, f"q6to{i}", [128, 1024], BF16) for i in range(2)])

    for blk in range(NDC):
        for hh in range(2):
            toq, b_toq = toutb[hh]
            dma("sync", toq[:].rearrange("p (c q) -> p c q", q=128),
                toutT_hbm[hh * 4096:(hh + 1) * 4096, blk * 128:(blk + 1) * 128]
                .rearrange("(c p) q -> p c q", p=128),
                ins=[b_toutc[blk // 8]], outs=[b_toq])
        for th in range(2):
            sl0 = th * 4
            ps_i = th * 2 + blk % 2
            for ncc in range(64):
                toq, b_toq = toutb[ncc // 32]
                nc2 = ncc % 32
                (T if ncc == 63 else Tq)(lambda e, ncc=ncc, nc2=nc2, sl0=sl0, ps_i=ps_i,
                                         toq=toq: e.matmul(
                    PS[ps_i][:], toq[:, nc2 * 128:(nc2 + 1) * 128],
                    bass_zslice(zv, sl0, ncc), start=(ncc == 0), stop=(ncc == 63)),
                    ins=[b_toq] + [b_zsl[sl0 + i] for i in range(4)], outs=[b_PS[ps_i]])
            yb, b_yb = ybf[th]
            Sc(lambda e, blk=blk, ps_i=ps_i, yb=yb: e.activation(
                yb[:], PS[ps_i][:], AF.Identity, bias=bout_sb[:, blk:blk + 1],
                scale=sclv[:, 5:6]), ins=[b_PS[ps_i], b_bout, b_sclv], outs=[b_yb])
            dma("gpsimd", y_hbm[blk * 128:(blk + 1) * 128, th * 512:(th + 1) * 512], yb[:],
                ins=[b_yb], outs=[b_yh])
        if blk < 8:
            drain_out_jobs(8)
    p.barrier()
    ph5.close()
    ph4.close()

    ph6 = ExitStack()
    sides[id(ph6)] = "right"
    yy, b_yy = sbp(ph6, "yy", [128, NDC * TOK], F32)
    mu4, b_mu4 = sbp(ph6, "mu4", [128, TOK], F32)
    r4, b_r4 = sbp(ph6, "r4", [128, TOK], F32)
    va4, b_va4 = sbp(ph6, "va4", [128, TOK], F32)
    mu5, b_mu5 = sbp(ph6, "mu5", [128, TOK], F32)
    r5, b_r5 = sbp(ph6, "r5", [128, TOK], F32)
    va5, b_va5 = sbp(ph6, "va5", [128, TOK], F32)
    sq6b = [sbp(ph6, f"sq6{i}", [128, 512], F32) for i in range(2)]
    x6b = [sbp(ph6, f"x6{i}", [128, TOK], F32) for i in range(2)]

    def lnstat(src_getter, b_srcs, mu, b_mu_, va, b_va_, r, b_r_, ps0, ps1):
        for dc in range(NDC):
            srcap, b_src = src_getter(dc)
            for qh in range(2):
                T(lambda e, dc=dc, qh=qh, srcap=srcap: e.matmul(
                    PS[ps0 + qh][:], ones_bf[:], srcap[:, qh * 512:(qh + 1) * 512],
                    start=(dc == 0), stop=(dc == NDC - 1)), ins=[b_src, b_ones], outs=[b_PS[ps0 + qh]])
                sq, b_sq = sq6b[qh]
                Sc(lambda e, srcap=srcap, qh=qh, sq=sq: e.activation(
                    sq[:], srcap[:, qh * 512:(qh + 1) * 512], AF.Square), ins=[b_src], outs=[b_sq])
                T(lambda e, sq=sq, dc=dc, qh=qh: e.matmul(PS[ps1 + qh][:], ones_bf[:], sq[:],
                                                   start=(dc == 0), stop=(dc == NDC - 1)),
                  ins=[b_sq, b_ones], outs=[b_PS[ps1 + qh]])
        for qh in range(2):
            V(lambda e, qh=qh: e.tensor_scalar(mu[:, qh * 512:(qh + 1) * 512], PS[ps0 + qh][:], 1.0 / D,
                                               None, op0=A.mult), ins=[b_PS[ps0 + qh]], outs=[b_mu_])
            V(lambda e, qh=qh: e.tensor_scalar(va[:, qh * 512:(qh + 1) * 512], PS[ps1 + qh][:], 1.0 / D,
                                               None, op0=A.mult), ins=[b_PS[ps1 + qh]], outs=[b_va_])
        V(lambda e: e.scalar_tensor_tensor(r[:], mu[:], 1.0, mu[:], op0=A.mult, op1=A.mult),
          ins=[b_mu_], outs=[b_r_])
        V(lambda e: e.tensor_tensor(va[:], va[:], r[:], op=A.subtract), ins=[b_r_, b_va_], outs=[b_va_])
        Sc(lambda e: e.activation(r[:], va[:], AF.Ln, bias=epsc[:, 0:1]), ins=[b_va_, b_eps], outs=[b_r_])
        Sc(lambda e: e.activation(r[:], r[:], AF.Exp, scale=-0.5), ins=[b_r_], outs=[b_r_])

    def getx(dc):
        xc, b_xc = x6b[dc % 2]
        dma("sync", xc[:], x1_hbm[dc * 128:(dc + 1) * 128, :], ins=[b_x1h], outs=[b_xc])
        return xc, b_xc
    lnstat(getx, None, mu5, b_mu5, va5, b_va5, r5, b_r5, 0, 2)

    for dc in range(NDC):
        xc, b_xc = x6b[dc % 2]
        dma("sync", xc[:], x1_hbm[dc * 128:(dc + 1) * 128, :], ins=[b_x1h], outs=[b_xc])
        V(lambda e, xc=xc: e.tensor_tensor(xc[:], xc[:], mu5[:], op=A.subtract),
          ins=[b_mu5], outs=[b_xc])
        V(lambda e, xc=xc: e.tensor_tensor(xc[:], xc[:], r5[:], op=A.mult), ins=[b_r5], outs=[b_xc])
        dma("sync", yy[:, dc * TOK:(dc + 1) * TOK], y_hbm[dc * 128:(dc + 1) * 128, :],
            ins=[b_yh], outs=[b_yy])
        V(lambda e, dc=dc, xc=xc: e.tensor_tensor(yy[:, dc * TOK:(dc + 1) * TOK],
                                                  yy[:, dc * TOK:(dc + 1) * TOK], xc[:], op=A.add),
          ins=[b_xc], outs=[b_yy])

    def getyy(dc):
        return yy[:, dc * TOK:(dc + 1) * TOK], b_yy
    lnstat(getyy, None, mu4, b_mu4, va4, b_va4, r4, b_r4, 0, 2)
    for dc in range(NDC):
        V(lambda e, dc=dc: e.tensor_tensor(yy[:, dc * TOK:(dc + 1) * TOK],
                                           yy[:, dc * TOK:(dc + 1) * TOK], mu4[:], op=A.subtract),
          ins=[b_mu4], outs=[b_yy])
        V(lambda e, dc=dc: e.tensor_tensor(yy[:, dc * TOK:(dc + 1) * TOK],
                                           yy[:, dc * TOK:(dc + 1) * TOK], r4[:], op=A.mult),
          ins=[b_r4], outs=[b_yy])
        V(lambda e, dc=dc: e.tensor_scalar(yy[:, dc * TOK:(dc + 1) * TOK],
                                           yy[:, dc * TOK:(dc + 1) * TOK],
                                           gam_sb[:, dc:dc + 1], bet_sb[:, dc:dc + 1],
                                           op0=A.mult, op1=A.add), ins=[b_gam, b_bet], outs=[b_yy])
    lnstat(getyy, None, mu5, b_mu5, va5, b_va5, r5, b_r5, 0, 2)
    for dc in range(NDC):
        xc, b_xc = x6b[dc % 2]
        dma("sync", xc[:], x1_hbm[dc * 128:(dc + 1) * 128, :], ins=[b_x1h], outs=[b_xc])
        V(lambda e, dc=dc: e.tensor_tensor(yy[:, dc * TOK:(dc + 1) * TOK],
                                           yy[:, dc * TOK:(dc + 1) * TOK], mu5[:], op=A.subtract),
          ins=[b_mu5], outs=[b_yy])
        V(lambda e, dc=dc: e.tensor_tensor(yy[:, dc * TOK:(dc + 1) * TOK],
                                           yy[:, dc * TOK:(dc + 1) * TOK], r5[:], op=A.mult),
          ins=[b_r5], outs=[b_yy])
        V(lambda e, dc=dc, xc=xc: e.tensor_tensor(xc[:], xc[:], yy[:, dc * TOK:(dc + 1) * TOK],
                                                  op=A.add), ins=[b_yy], outs=[b_xc])
        dma("gpsimd", y_out[dc * 128:(dc + 1) * 128, :], xc[:], ins=[b_xc], outs=[b_yo])
    p.barrier()
    ph6.close()
    psS.close()
    stP.close()
    return nc, p


def bass_zslice(zv, sl0, ncc):
    return zv[:, sl0:sl0 + 4, ncc * 128:(ncc + 1) * 128]


_PROG_CACHE = {}


def _get_program():
    if "nc" not in _PROG_CACHE:
        nc, p = build_program()
        p.build()
        _PROG_CACHE["nc"] = nc
    return _PROG_CACHE["nc"]


def _host_inputs(inputs):
    import ml_dtypes
    x = np.asarray(inputs["L0_attn_hidden_states"])
    wq = np.asarray(inputs["L0_attn_Wq"])
    wk = np.asarray(inputs["L0_attn_Wk"])
    wv = np.asarray(inputs["L0_attn_Wv"])
    wo = np.asarray(inputs["L0_attn_Wo"])
    bo = np.asarray(inputs["L0_attn_bo"])
    wsyn = np.asarray(inputs["L0_ffn_W_syn"])
    bsyn = np.asarray(inputs["L0_ffn_b_syn"])
    wout = np.asarray(inputs["L0_ffn_W_out"])
    bout = np.asarray(inputs["L0_ffn_b_out"])
    gam = np.asarray(inputs["L0_ffn_gamma"])
    bet = np.asarray(inputs["L0_ffn_beta"])

    shared = {
        "wqT": np.ascontiguousarray(wq.T),
        "wkT": np.ascontiguousarray(wk.T),
        "wvT": np.ascontiguousarray(wv.T),
        "woT": np.ascontiguousarray(wo.T),
        "wsynT": np.ascontiguousarray(wsyn.T),
        "woutT": np.ascontiguousarray(wout.T),
        "bo": np.ascontiguousarray(bo.reshape(NDC, 128)),
        "bout": np.ascontiguousarray(bout.reshape(NDC, 128)),
        "gam": np.ascontiguousarray(gam.reshape(NDC, 128)),
        "bet": np.ascontiguousarray(bet.reshape(NDC, 128)),
        "bsyn": np.ascontiguousarray(bsyn.reshape(NNB, 512)),
        "diag01": np.eye(128, dtype=ml_dtypes.bfloat16),
    }
    in_maps = []
    perms = []
    for c in range(8):
        b, half = c // 2, c % 2
        ch = CHUNKS[half]
        perm = np.concatenate([np.arange(cc * 128, cc * 128 + 128) for cc in ch])
        perms.append(perm)
        xb = x[b].T
        cm = np.zeros((8, 128, 256), np.float32)
        for qt in range(8):
            cq = ch[qt]
            keys = 256 * qt + np.arange(256)[None, :]
            qpos = cq * 128 + np.arange(128)[:, None]
            cm[qt] = np.where(keys <= qpos, 0.0, NEG)
        m = dict(shared)
        m["xTs"] = np.ascontiguousarray(xb)
        m["xTo"] = np.ascontiguousarray(xb[:, perm])
        m["cmask"] = cm
        in_maps.append(m)
    return in_maps, perms


def kernel(**inputs):
    nc = _get_program()
    in_maps, perms = _host_inputs(inputs)
    res = run_bass_kernel_spmd(nc, in_maps, core_ids=list(range(8)))
    x = np.asarray(inputs["L0_attn_hidden_states"])
    out = np.empty((4, S, D), np.float32)
    for c in range(8):
        y = np.asarray(res.results[c]["y"])
        out[c // 2, perms[c], :] = y.T
    return out


# version 21
# speedup vs baseline: 1.0262x; 1.0262x over previous
import numpy as np
from contextlib import ExitStack

import concourse.bass as bass
import concourse.mybir as mybir
from concourse.bass_utils import run_bass_kernel_spmd
from concourse.alu_op_type import AluOpType as A

F32 = mybir.dt.float32
BF16 = mybir.dt.bfloat16
AF = mybir.ActivationFunctionType

D = 2048
S = 2048
NH, NKV, HD = 16, 4, 128
N = 8192
K_TOP = 819
EPS = 1e-5
NEG = -1.0e30
TOK = 1024
NDC = 16
NSL = 8
NNB = 16
BITERS = 11
CHUNKS = [[0, 2, 4, 6, 9, 11, 13, 15], [1, 3, 5, 7, 8, 10, 12, 14]]
ENG = ("sync", "vector", "scalar", "tensor", "gpsimd")
OWN_SEM = {"vector": "V", "scalar": "Sc", "tensor": "T"}


class Buf:
    __slots__ = ("w", "rs", "dsem")

    def __init__(self):
        self.w = None
        self.rs = []
        self.dsem = None


class Plan:
    def __init__(self, nc):
        self.nc = nc
        self.items = {e: [] for e in ENG}
        self.sem_names = []
        self.cnt = {}
        self.marks = {e: {} for e in ENG}

    def sem(self, name):
        self.sem_names.append(name)
        self.cnt[name] = 0

    def dma_sem(self, b):
        if not hasattr(b, "dsem") or b.dsem is None:
            b.dsem = f"d{len(self.sem_names)}"
            self.sem(b.dsem)
        return b.dsem

    def emit(self, eng, fn, ins=(), outs=(), sem=None, n=1, extra=(), quiet=False):
        is_load = n == 16 and not ins
        reads = []
        for b in ins:
            if b.w is not None:
                reads.append(b.w)
        for b in outs:
            if b.w is not None and not (is_load and b.w[0] == sem and not b.rs):
                reads.append(b.w)
            reads.extend(b.rs)
        reads.extend(t for t in extra if t is not None)
        waits = []
        own = OWN_SEM.get(eng)
        for (s, v) in reads:
            if eng == "tensor" and s == own:
                continue
            if v > 0 and self.marks[eng].get(s, 0) < v:
                waits.append((s, v))
                self.marks[eng][s] = v
        if quiet:
            tok = (sem, self.cnt[sem] + n)
            self.items[eng].append((waits, fn, None, n, None))
        else:
            self.cnt[sem] += n
            tok = (sem, self.cnt[sem])
            self.items[eng].append((waits, fn, sem, n, self.cnt[sem]))
        for b in ins:
            b.rs.append(tok)
        for b in outs:
            b.w = tok
            b.rs = []
        return tok

    def barrier(self):
        toks = [(s, self.cnt[s]) for s in self.sem_names]
        for e in ENG:
            for (s, v) in toks:
                if v > 0 and self.marks[e].get(s, 0) < v:
                    self.items[e].append(([(s, v)], None, None, 1, None))
                    self.marks[e][s] = v

    def build(self):
        nc = self.nc
        with ExitStack() as ses:
            sems = {s: ses.enter_context(nc.semaphore(s)) for s in self.sem_names}
            with nc.Block() as block:
                def mk(eng_name):
                    def run(eng):
                        chk = {}
                        for (waits, fn, sem, n, pv) in self.items[eng_name]:
                            for (s, v) in waits:
                                eng.wait_ge(sems[s], v)
                            if fn is None:
                                continue
                            instr = fn(eng)
                            if sem is None:
                                continue
                            instr.then_inc(sems[sem], n)
                            chk[sem] = chk.get(sem, 0) + n
                            assert chk[sem] == pv or pv is None, (eng_name, sem, chk[sem], pv)
                    return run
                for e in ENG:
                    if self.items[e]:
                        getattr(block, e)(mk(e))


def build_program(stop_after=None):
    nc = bass.Bass("TRN2", target_bir_lowering=False)
    p = Plan(nc)
    for s in ("V", "Sc", "T"):
        p.sem(s)

    dr = nc.dram_tensor
    xTs = dr("xTs", [D, S], F32, kind="ExternalInput").ap()
    xTo = dr("xTo", [D, TOK], F32, kind="ExternalInput").ap()
    wqT = dr("wqT", [D, NH * HD], F32, kind="ExternalInput").ap()
    wkT = dr("wkT", [D, NKV * HD], F32, kind="ExternalInput").ap()
    wvT = dr("wvT", [D, NKV * HD], F32, kind="ExternalInput").ap()
    woT = dr("woT", [NH * HD, D], F32, kind="ExternalInput").ap()
    wsynT_h = dr("wsynT", [D, N], F32, kind="ExternalInput").ap()
    woutT_h = dr("woutT", [N, D], F32, kind="ExternalInput").ap()
    bo_h = dr("bo", [NDC, 128], F32, kind="ExternalInput").ap()
    bout_h = dr("bout", [NDC, 128], F32, kind="ExternalInput").ap()
    gam_h = dr("gam", [NDC, 128], F32, kind="ExternalInput").ap()
    bet_h = dr("bet", [NDC, 128], F32, kind="ExternalInput").ap()
    bsyn_h = dr("bsyn", [NNB, 512], F32, kind="ExternalInput").ap()
    diag01_h = dr("diag01", [128, 128], BF16, kind="ExternalInput").ap()
    cmask_h = dr("cmask", [8, 128, 256], F32, kind="ExternalInput").ap()
    x1_hbm = dr("x1scr", [D, TOK], F32, kind="Internal").ap()
    tqT_hbm = dr("tqscr", [D, N], BF16, kind="Internal").ap()
    toutT_hbm = dr("toscr", [N, D], BF16, kind="Internal").ap()
    y_hbm = dr("yscr", [D, TOK], F32, kind="Internal").ap()
    y_out = dr("y", [D, TOK], F32, kind="ExternalOutput").ap()
    b_x1h, b_yh, b_yo, b_tqh = Buf(), Buf(), Buf(), Buf()

    stP = ExitStack()

    sides = {}

    def sbp(es, name, shape, dt):
        side = sides.get(id(es), "left")
        return es.enter_context(nc.sbuf_tensor(name, shape, dt, side=side)), Buf()

    ones_bf, b_ones = sbp(stP, "ones_f32", [128, 128], F32)
    ones1_bf, b_ones1 = sbp(stP, "ones1_bf", [1, 512], BF16)
    diag01, b_diag01 = sbp(stP, "diag01_sb", [128, 128], BF16)
    bo_sb, b_bo = sbp(stP, "bo_sb", [128, NDC], F32)
    bout_sb, b_bout = sbp(stP, "bout_sb", [128, NDC], F32)
    gam_sb, b_gam = sbp(stP, "gam_sb", [128, NDC], F32)
    bet_sb, b_bet = sbp(stP, "bet_sb", [128, NDC], F32)
    bsyn_sl, b_bsyn = sbp(stP, "bsyn_sl", [1, 512], BF16)
    iota8, b_iota = sbp(stP, "iota8", [128, 8], F32)
    sclv, b_sclv = sbp(stP, "sclv", [128, 8], F32)
    sprt, b_sprt = sbp(stP, "sprt", [128, 128], F32)
    stt, b_stt = sbp(stP, "stt", [128, 12 * NSL], F32)
    m8t, b_m8 = sbp(stP, "m8t", [128, 8 * NSL], F32)
    musum, b_mus = sbp(stP, "musum", [128, NSL * NNB], F32)
    small, b_small = sbp(stP, "small", [128, 32], F32)
    epsc, b_eps = sbp(stP, "epsc", [128, 2], F32)

    psS = ExitStack()
    PS = [psS.enter_context(nc.psum_tensor(f"ps{i}", [128, 512], F32)) for i in range(6)]
    b_PS = [Buf() for _ in range(6)]
    PTb = [psS.enter_context(nc.psum_tensor(f"pt{i}", [128, 512], F32)) for i in range(2)]
    b_PT = [Buf() for _ in range(2)]

    def dma(q, dst, src, ins=(), outs=(), sem=None, extra=()):
        sm = p.dma_sem(outs[0])
        return p.emit(q, lambda e, d=dst, s=src: e.dma_start(d, s), ins=ins, outs=outs,
                      sem=sm, n=16, extra=extra)

    def V(fn, ins=(), outs=(), extra=()):
        return p.emit("vector", fn, ins=ins, outs=outs, sem="V", extra=extra)

    def Sc(fn, ins=(), outs=(), extra=()):
        return p.emit("scalar", fn, ins=ins, outs=outs, sem="Sc", extra=extra)

    def T(fn, ins=(), outs=(), extra=()):
        return p.emit("tensor", fn, ins=ins, outs=outs, sem="T", extra=extra)

    def Tq(fn, ins=(), outs=(), extra=()):
        return p.emit("tensor", fn, ins=ins, outs=outs, sem="T", extra=extra, quiet=True)

    dma("sync", diag01[:], diag01_h, outs=[b_diag01])
    for ci in range(NDC):
        dma("sync", bo_sb[:, ci:ci + 1], bo_h[ci:ci + 1, :].rearrange("a b -> b a"), outs=[b_bo])
        dma("sync", bout_sb[:, ci:ci + 1], bout_h[ci:ci + 1, :].rearrange("a b -> b a"), outs=[b_bout])
        dma("sync", gam_sb[:, ci:ci + 1], gam_h[ci:ci + 1, :].rearrange("a b -> b a"), outs=[b_gam])
        dma("sync", bet_sb[:, ci:ci + 1], bet_h[ci:ci + 1, :].rearrange("a b -> b a"), outs=[b_bet])
    V(lambda e: e.memset(ones_bf[:], 1.0), outs=[b_ones])
    V(lambda e: e.memset(ones1_bf[:], 1.0), outs=[b_ones1])
    for i in range(8):
        V(lambda e, i=i: e.memset(iota8[:, i:i + 1], float(i)), outs=[b_iota])
    V(lambda e: e.memset(sprt[:], 0.0), outs=[b_sprt])
    V(lambda e: e.memset(epsc[:, 0:1], EPS), outs=[b_eps])
    V(lambda e: e.memset(epsc[:, 1:2], 1e-12), outs=[b_eps])

    ph1 = ExitStack()
    sides[id(ph1)] = "right"
    xcb1 = [sbp(ph1, f"xc1_{i}", [128, 512], F32) for i in range(4)]
    ln1gb = [sbp(ph1, f"ln1g{i}", [128, NDC * 512], BF16) for i in range(2)]
    sqb = [sbp(ph1, f"sq{i}", [128, 512], F32) for i in range(2)]
    mu1b = [sbp(ph1, f"mu1_{i}", [128, 512], F32) for i in range(2)]
    r1b = [sbp(ph1, f"r1_{i}", [128, 512], F32) for i in range(2)]
    va1b = [sbp(ph1, f"va1_{i}", [128, 512], F32) for i in range(2)]
    wk_bf, b_wk = sbp(ph1, "wk_bf", [128, NDC * 512], BF16)
    wv_bf, b_wv = sbp(ph1, "wv_bf", [128, NDC * 512], BF16)
    wqb = [sbp(ph1, f"wq_bf{i}", [128, NDC * 128], BF16) for i in range(2)]

    phA = ExitStack()
    kbf, b_kbf = sbp(phA, "kbf", [128, NKV * S], BF16)
    vbf, b_vbf = sbp(phA, "vbf", [128, 16 * 512], BF16)
    qbf, b_qbf = sbp(phA, "qbf", [128, NH * TOK], BF16)
    cmask, b_cm = sbp(phA, "cmask_sb", [128, 8 * 256], F32)
    wsfb = [sbp(phA, f"wsf{i}", [128, 2048], F32) for i in range(2)]
    for qt_ in range(8):
        dma("sync", cmask[:, qt_ * 256:(qt_ + 1) * 256], cmask_h[qt_, :, :], outs=[b_cm])

    for dc in range(NDC):
        dma("gpsimd", wk_bf[:, dc * 512:(dc + 1) * 512], wkT[dc * 128:(dc + 1) * 128, :],
            outs=[b_wk])
        dma("gpsimd", wv_bf[:, dc * 512:(dc + 1) * 512], wvT[dc * 128:(dc + 1) * 128, :],
            outs=[b_wv])

    scale_jobs = []
    for (src_ap, cbase) in ((wsynT_h, 0), (woutT_h, 64)):
        nr, ncc = src_ap.shape[0] // 128, src_ap.shape[1] // 2048
        for r in range(nr):
            for c in range(ncc):
                def job(r=r, c=c, src_ap=src_ap, cbase=cbase, idx=r * ncc + c):
                    wb, bb = wsfb[idx % 2]
                    dma("sync", wb[:, 0:2048], src_ap[r * 128:(r + 1) * 128, c * 2048:(c + 1) * 2048],
                        outs=[bb])
                    Sc(lambda e, wb=wb, col=cbase + idx: e.activation(
                        wb[:, 0:2048], wb[:, 0:2048], AF.Abs, accum_out=sprt[:, col:col + 1]),
                       ins=[bb], outs=[bb, b_sprt])
                scale_jobs.append(job)
    scale_jobs.reverse()

    def drain_scale_jobs(k):
        for _ in range(min(k, len(scale_jobs))):
            scale_jobs.pop()()

    def finish_scale(base, tcol):
        cb = 0 if base == 0 else 64
        V(lambda e, cb=cb: e.tensor_scalar(
            small[:, 0:1], sprt[:, cb:cb + 64].rearrange("p a -> p a") if False else small[:, 0:1],
            1.0, None, op0=A.mult), ins=[], outs=[]) if False else None
        V(lambda e, cb=cb: e.tensor_scalar(
            sprt[:, cb:cb + 64], sprt[:, cb:cb + 64], 1.0, None, op0=A.mult, op1=A.add,
            accum_out=small[:, 0:1]), ins=[b_sprt], outs=[b_sprt, b_small])
        V(lambda e, cb=cb: e.tensor_scalar(
            sprt[:, cb:cb + 1], small[:, 0:1], 1.0, None, op0=A.mult),
          ins=[b_small], outs=[b_sprt])
        T(lambda e, cb=cb: e.matmul(PS[5][:, 0:1], ones_bf[:], sprt[:, cb:cb + 1],
                             start=True, stop=True), ins=[b_sprt, b_ones], outs=[b_PS[5]])
        V(lambda e: e.tensor_scalar(sclv[:, base + 2:base + 3], PS[5][:, 0:1], 1.0 / (D * N), None,
                                    op0=A.mult), ins=[b_PS[5]], outs=[b_sclv])
        V(lambda e: e.tensor_scalar(sclv[:, base:base + 1], sclv[:, base + 2:base + 3],
                                    0.5, 0.5e-8, op0=A.mult, op1=A.add), ins=[b_sclv], outs=[b_sclv])
        V(lambda e: e.tensor_scalar(sclv[:, base + 1:base + 2], sclv[:, base:base + 1],
                                    -1.0, None, op0=A.mult), ins=[b_sclv], outs=[b_sclv])

    quant_jobs = []
    out_jobs = []
    QB = {"cnt": [0]}
    b_tqcol = [Buf() for _ in range(N // 1024)]
    b_toutc = [Buf() for _ in range(D // 1024)]

    def mk_qjob(src_ap, dst_ap, r, c, base, dst_buf):
        def qjob():
            i = QB["cnt"][0] % 2
            QB["cnt"][0] += 1
            qwf, b_qwf = QB["qwfb"][i]
            qto, b_qto = QB["qtob"][i]
            qab, b_qa = QB["qab"]
            dma("sync", qwf[:], src_ap[r * 128:(r + 1) * 128, c * 1024:(c + 1) * 1024],
                outs=[b_qwf])
            Sc(lambda e, qwf=qwf, qto=qto: e.activation(qto[:], qwf[:], AF.Sign,
                                                        bias=sclv[:, base + 1:base + 2]),
               ins=[b_qwf, b_sclv], outs=[b_qto])
            Sc(lambda e, qwf=qwf, qab=qab: e.activation(qab[:], qwf[:], AF.Sign,
                                                        bias=sclv[:, base:base + 1]),
               ins=[b_qwf, b_sclv], outs=[b_qa])
            V(lambda e, qab=qab: e.tensor_scalar(qab[:], qab[:], 0.0, None, op0=A.min),
              ins=[b_qa], outs=[b_qa])
            V(lambda e, qto=qto, qab=qab: e.scalar_tensor_tensor(qto[:], qto[:], 0.0, qab[:],
                                                                 op0=A.max, op1=A.add),
              ins=[b_qa], outs=[b_qto])
            dma("gpsimd", dst_ap[r * 128:(r + 1) * 128, c * 1024:(c + 1) * 1024], qto[:],
                ins=[b_qto], outs=[dst_buf])
        return qjob

    for qc in range(N // 1024):
        for qr in range(D // 128):
            quant_jobs.append(mk_qjob(wsynT_h, tqT_hbm, qr, qc, 0, b_tqcol[qc]))
    for qc in range(D // 1024):
        for qr in range(N // 128):
            out_jobs.append(mk_qjob(woutT_h, toutT_hbm, qr, qc, 3, b_toutc[qc]))

    def drain_quant_jobs(k):
        for _ in range(min(k, len(quant_jobs))):
            quant_jobs.pop(0)()

    def drain_quant_to(k):
        while 128 - len(quant_jobs) < k and quant_jobs:
            quant_jobs.pop(0)()

    def drain_out_jobs(k):
        for _ in range(min(k, len(out_jobs))):
            out_jobs.pop(0)()

    def ln1_group(x_src, t0, gpar):
        ln1g, b_ln1 = ln1gb[gpar]
        mu1, b_mu1 = mu1b[gpar]
        r1, b_r1 = r1b[gpar]
        va1, b_va1 = va1b[gpar]
        for dc in range(NDC):
            xc, b_xc = xcb1[dc % 2]
            dma("sync", xc[:], x_src[dc * 128:(dc + 1) * 128, t0:t0 + 512], outs=[b_xc])
            T(lambda e, xc=xc, dc=dc: e.matmul(PS[0][:], ones_bf[:], xc[:],
                                               start=(dc == 0), stop=(dc == NDC - 1)),
              ins=[b_xc, b_ones], outs=[b_PS[0]])
            sq, b_sq = sqb[dc % 2]
            Sc(lambda e, xc=xc, sq=sq: e.activation(sq[:], xc[:], AF.Square),
               ins=[b_xc], outs=[b_sq])
            T(lambda e, sq=sq, dc=dc: e.matmul(PS[1][:], ones_bf[:], sq[:],
                                               start=(dc == 0), stop=(dc == NDC - 1)),
              ins=[b_sq, b_ones], outs=[b_PS[1]])
        V(lambda e, mu1=mu1: e.tensor_scalar(mu1[:], PS[0][:], 1.0 / D, None, op0=A.mult),
          ins=[b_PS[0]], outs=[b_mu1])
        V(lambda e, va1=va1: e.tensor_scalar(va1[:], PS[1][:], 1.0 / D, None, op0=A.mult),
          ins=[b_PS[1]], outs=[b_va1])
        sq, b_sq = sqb[0]
        V(lambda e, sq=sq, mu1=mu1: e.scalar_tensor_tensor(sq[:], mu1[:], 1.0, mu1[:],
                                                           op0=A.mult, op1=A.mult),
          ins=[b_mu1], outs=[b_sq])
        V(lambda e, sq=sq, va1=va1: e.tensor_tensor(va1[:], va1[:], sq[:], op=A.subtract),
          ins=[b_sq, b_va1], outs=[b_va1])
        Sc(lambda e, r1=r1, va1=va1: e.activation(r1[:], va1[:], AF.Ln, bias=epsc[:, 0:1]),
           ins=[b_va1, b_eps], outs=[b_r1])
        Sc(lambda e, r1=r1: e.activation(r1[:], r1[:], AF.Exp, scale=-0.5), ins=[b_r1], outs=[b_r1])
        for dc in range(NDC):
            xc, b_xc = xcb1[2 + dc % 2]
            dma("sync", xc[:], x_src[dc * 128:(dc + 1) * 128, t0:t0 + 512], outs=[b_xc])
            V(lambda e, xc=xc, mu1=mu1: e.tensor_tensor(xc[:], xc[:], mu1[:], op=A.subtract),
              ins=[b_mu1], outs=[b_xc])
            V(lambda e, dc=dc, xc=xc, r1=r1, ln1g=ln1g: e.tensor_tensor(
                ln1g[:, dc * 512:(dc + 1) * 512], xc[:], r1[:], op=A.mult),
                ins=[b_xc, b_r1], outs=[b_ln1])
        return ln1g, b_ln1

    for g in range(4):
        ln1g, b_ln1 = ln1_group(xTs, g * 512, g % 2)
        for kb in range(NKV):
            for dc in range(NDC):
                (T if dc == NDC - 1 else Tq)(lambda e, kb=kb, dc=dc, ln1g=ln1g: e.matmul(
                    PS[2][:], wk_bf[:, dc * 512 + kb * 128: dc * 512 + kb * 128 + 128],
                    ln1g[:, dc * 512:(dc + 1) * 512],
                    start=(dc == 0), stop=(dc == NDC - 1)),
                    ins=[b_ln1, b_wk], outs=[b_PS[2]])
            Sc(lambda e, kb=kb, g=g: e.copy(kbf[:, kb * S + g * 512: kb * S + g * 512 + 512],
                                            PS[2][:]), ins=[b_PS[2]], outs=[b_kbf])
        for slc in range(4):
            for dc in range(NDC):
                (T if dc == NDC - 1 else Tq)(lambda e, slc=slc, dc=dc, ln1g=ln1g: e.matmul(
                    PS[3][:], ln1g[:, dc * 512 + slc * 128: dc * 512 + slc * 128 + 128],
                    wv_bf[:, dc * 512:(dc + 1) * 512],
                    start=(dc == 0), stop=(dc == NDC - 1)),
                    ins=[b_ln1, b_wv], outs=[b_PS[3]])
            Sc(lambda e, kt=g * 4 + slc: e.copy(vbf[:, kt * 512:(kt + 1) * 512], PS[3][:]),
               ins=[b_PS[3]], outs=[b_vbf])
        drain_scale_jobs(8)

    for g in range(2):
        ln1g, b_ln1 = ln1_group(xTo, g * 512, g % 2)
        for hb in range(NH):
            wq, b_wq = wqb[hb % 2]
            dma("gpsimd", wq[:].rearrange("p (c q) -> p c q", q=128),
                wqT[:, hb * 128:(hb + 1) * 128].rearrange("(c p) q -> p c q", p=128),
                outs=[b_wq])
            for dc in range(NDC):
                (T if dc == NDC - 1 else Tq)(lambda e, dc=dc, wq=wq, ln1g=ln1g: e.matmul(
                    PS[4][:], wq[:, dc * 128:(dc + 1) * 128], ln1g[:, dc * 512:(dc + 1) * 512],
                    start=(dc == 0), stop=(dc == NDC - 1)),
                    ins=[b_ln1, b_wq], outs=[b_PS[4]])
            Sc(lambda e, hb=hb, g=g: e.copy(qbf[:, hb * TOK + g * 512: hb * TOK + g * 512 + 512],
                                            PS[4][:]), ins=[b_PS[4]], outs=[b_qbf])
            drain_scale_jobs(2)
    finish_scale(0, 1)

    p.barrier()
    if stop_after == "p1":
        return nc, p
    ph1.close()

    ph2 = ExitStack()
    sides[id(ph2)] = "right"
    p_bfb = [sbp(ph2, f"p_bf{i}", [128, 2048], BF16) for i in range(2)]
    pT, b_pT = sbp(ph2, "pT", [128, 16 * TOK], BF16)
    a_bf, b_a = sbp(ph2, "a_bf", [128, NH * TOK], BF16)
    diag_rb = [sbp(ph2, f"diag_r{i}", [128, 512], BF16) for i in range(2)]
    junk_sb = [sbp(ph2, f"junk_s{i}", [128, 512], BF16) for i in range(2)]
    mxsb = []
    for ii in range(2):
        _mt, _ = sbp(ph2, f"mxs{ii}", [128, 8], F32)
        mxsb.append((_mt, [Buf() for _ in range(4)], Buf()))
    smallb = [sbp(ph2, f"small2{i}", [128, 16], F32) for i in range(2)]
    pTv = pT[:, 0:16 * TOK].rearrange("p (k t) -> p k t", t=TOK)

    for kt in range(16):
        qmin = (kt // 2) * 128
        if qmin > 0:
            V(lambda e, kt=kt, qmin=qmin: e.memset(pT[:, kt * TOK: kt * TOK + qmin], 0.0),
              outs=[b_pT])

    rrc = [0]
    its = [(h, qt) for h in range(NH) for qt in range(NSL)]

    def st1(i):
        h, qt = its[i]
        kv = h // 4
        pbuf, b_pb = p_bfb[i % 2]
        junk_s, b_js = junk_sb[i % 2]
        mxs, mxc, b_mxg = mxsb[i % 2]
        sm2, b_sm2 = smallb[i % 2]
        kmax = 256 * (qt + 1)
        nk5 = (kmax + 511) // 512
        for j in range(nk5):
            w = min(512, kmax - j * 512)
            r = rrc[0] % 4
            rrc[0] += 1
            psb = b_PS[r]
            T(lambda e, h=h, qt=qt, j=j, w=w, kv=kv, r=r: e.matmul(
                PS[r][:, 0:w], qbf[:, h * TOK + qt * 128: h * TOK + qt * 128 + 128],
                kbf[:, kv * S + j * 512: kv * S + j * 512 + w], start=True, stop=True),
                ins=[b_qbf, b_kbf], outs=[psb])
            if j == nk5 - 1:
                V(lambda e, w=w, r=r, qt=qt: e.tensor_tensor(
                    PS[r][:, w - 256:w], PS[r][:, w - 256:w],
                    cmask[:, qt * 256:(qt + 1) * 256], op=A.add), ins=[b_cm, psb], outs=[psb])
            V(lambda e, j=j, w=w, r=r, junk_s=junk_s, mxs=mxs: e.tensor_scalar(
                junk_s[:, 0:w], PS[r][:, 0:w], -1.0, None, op0=A.mult, op1=A.min,
                accum_out=mxs[:, j:j + 1]), ins=[psb], outs=[b_js, mxc[j]])
            Sc(lambda e, j=j, w=w, r=r, pbuf=pbuf, mxs=mxs, sm2=sm2: e.activation(
                pbuf[:, j * 512: j * 512 + w], PS[r][:, 0:w], AF.Exp,
                bias=mxs[:, j:j + 1], accum_out=sm2[:, j:j + 1]),
                ins=[psb, mxc[j]], outs=[b_pb, b_sm2])

    def st2(i):
        h, qt = its[i]
        pbuf, b_pb = p_bfb[i % 2]
        diag_r, b_drg = diag_rb[i % 2]
        mxs, mxc, b_mxg = mxsb[i % 2]
        sm2, b_sm2 = smallb[i % 2]
        kmax = 256 * (qt + 1)
        nk5 = (kmax + 511) // 512
        if nk5 > 1:
            V(lambda e, nk5=nk5, mxs=mxs: e.tensor_scalar(
                mxs[:, 4:4 + nk5], mxs[:, 0:nk5], 1.0, None, op0=A.mult, op1=A.min,
                accum_out=mxs[:, 7:8]), ins=mxc[0:nk5], outs=[b_mxg])
        else:
            V(lambda e, mxs=mxs: e.tensor_scalar(mxs[:, 7:8], mxs[:, 0:1], 1.0, None,
                                                 op0=A.mult), ins=mxc[0:1], outs=[b_mxg])
        Sc(lambda e, nk5=nk5, mxs=mxs: e.activation(mxs[:, 4:4 + nk5], mxs[:, 0:nk5],
                                                    AF.Exp, scale=-1.0, bias=mxs[:, 7:8]),
           ins=mxc[0:nk5] + [b_mxg], outs=[b_mxg])
        V(lambda e, nk5=nk5, sm2=sm2, mxs=mxs: e.scalar_tensor_tensor(
            sm2[:, 8:8 + nk5], sm2[:, 0:nk5], 1.0, mxs[:, 4:4 + nk5],
            op0=A.mult, op1=A.mult, accum_out=sm2[:, 12:13]),
            ins=[b_sm2, b_mxg], outs=[b_sm2])
        V(lambda e, sm2=sm2: e.reciprocal(sm2[:, 13:14], sm2[:, 12:13]),
          ins=[b_sm2], outs=[b_sm2])
        V(lambda e, nk5=nk5, sm2=sm2, mxs=mxs: e.tensor_scalar(
            sm2[:, 8:8 + nk5], mxs[:, 4:4 + nk5], sm2[:, 13:14], None,
            op0=A.mult), ins=[b_sm2, b_mxg], outs=[b_sm2])
        for j in range(nk5):
            V(lambda e, j=j, diag_r=diag_r, sm2=sm2: e.tensor_scalar(
                diag_r[:, j * 128:(j + 1) * 128], diag01[:], sm2[:, 8 + j:9 + j],
                None, op0=A.mult), ins=[b_diag01, b_sm2], outs=[b_drg])
        nkb = 2 * (qt + 1)
        for kt0 in range(0, nkb, 4):
            nk = min(4, nkb - kt0)
            pt_i = (kt0 // 4) % 2
            for k2 in range(nk):
                jblk = (kt0 + k2) // 4
                T(lambda e, k2=k2, kt0=kt0, pt_i=pt_i, pbuf=pbuf, diag_r=diag_r,
                  jblk=jblk: e.matmul(
                    PTb[pt_i][:, k2 * 128:(k2 + 1) * 128],
                    pbuf[:, (kt0 + k2) * 128:(kt0 + k2 + 1) * 128],
                    diag_r[:, jblk * 128:(jblk + 1) * 128],
                    start=True, stop=True),
                    ins=[b_pb, b_drg], outs=[b_PT[pt_i]])
            Sc(lambda e, kt0=kt0, nk=nk, qt=qt, pt_i=pt_i: e.copy(
                pTv[:, kt0:kt0 + nk, qt * 128:(qt + 1) * 128],
                PTb[pt_i][:, 0:nk * 128].rearrange("p (k t) -> p k t", t=128)),
               ins=[b_PT[pt_i]], outs=[b_pT])

    def pv(h):
        kv = h // 4
        for qh in range(2):
            for kt in range(16):
                (T if kt == 15 else Tq)(lambda e, qh=qh, kt=kt, kv=kv: e.matmul(
                    PS[4 + qh][:], vbf[:, kt * 512 + kv * 128: kt * 512 + kv * 128 + 128],
                    pT[:, kt * TOK + qh * 512: kt * TOK + qh * 512 + 512],
                    start=(kt == 0), stop=(kt == 15)),
                    ins=[b_vbf, b_pT], outs=[b_PS[4 + qh]])
            Sc(lambda e, h=h, qh=qh: e.copy(a_bf[:, h * TOK + qh * 512: h * TOK + qh * 512 + 512],
                                            PS[4 + qh][:]), ins=[b_PS[4 + qh]], outs=[b_a])

    for i in range(len(its) + 1):
        if i > 0:
            st2(i - 1)
            if its[i - 1][1] == NSL - 1:
                pv(its[i - 1][0])
                if stop_after == ("p2pv", its[i - 1][0]):
                    p.barrier()
                    return nc, p
        if i < len(its):
            st1(i)
        if i % 4 == 0:
            drain_scale_jobs(1)
    drain_scale_jobs(len(scale_jobs))

    p.barrier()
    if stop_after == "p2":
        return nc, p
    phA.close()

    ph3 = ExitStack()
    sides[id(ph3)] = "right"
    wob = [sbp(ph3, f"wo_bf{i}", [128, NDC * 128], BF16) for i in range(2)]
    attn, b_at = sbp(ph3, "attn", [128, NDC * TOK], F32)
    sq3b = [sbp(ph3, f"sq3{i}", [128, 512], F32) for i in range(2)]
    mu2, b_mu2 = sbp(ph3, "mu2", [128, TOK], F32)
    r2, b_r2 = sbp(ph3, "r2", [128, TOK], F32)
    va2, b_va2 = sbp(ph3, "va2", [128, TOK], F32)
    xcb = [sbp(ph3, f"xc{i}", [128, TOK], F32) for i in range(2)]
    QB.update(qwfb=[sbp(ph3, f"q3wf{i}", [128, 1024], F32) for i in range(2)],
              qab=sbp(ph3, "q3a", [128, 1024], BF16),
              qtob=[sbp(ph3, f"q3to{i}", [128, 1024], BF16) for i in range(2)])
    finish_scale(3, 3)

    for blk in range(NDC):
        wo, b_wo = wob[blk % 2]
        dma("gpsimd", wo[:].rearrange("p (c q) -> p c q", q=128),
            woT[:, blk * 128:(blk + 1) * 128].rearrange("(c p) q -> p c q", p=128),
            outs=[b_wo])
        for qh in range(2):
            for dc in range(NDC):
                (T if dc == NDC - 1 else Tq)(lambda e, qh=qh, dc=dc, wo=wo: e.matmul(
                    PS[qh][:], wo[:, dc * 128:(dc + 1) * 128],
                    a_bf[:, dc * TOK + qh * 512: dc * TOK + qh * 512 + 512],
                    start=(dc == 0), stop=(dc == NDC - 1)),
                    ins=[b_a, b_wo], outs=[b_PS[qh]])
            Sc(lambda e, blk=blk, qh=qh: e.activation(
                attn[:, blk * TOK + qh * 512: blk * TOK + qh * 512 + 512], PS[qh][:], AF.Identity,
                bias=bo_sb[:, blk:blk + 1]), ins=[b_PS[qh], b_bo], outs=[b_at])
            drain_quant_jobs(1)
            drain_out_jobs(1)
    for qh in range(2):
        for dc in range(NDC):
            T(lambda e, dc=dc, qh=qh: e.matmul(
                PS[qh][:], ones_bf[:], attn[:, dc * TOK + qh * 512: dc * TOK + qh * 512 + 512],
                start=(dc == 0), stop=(dc == NDC - 1)), ins=[b_at, b_ones], outs=[b_PS[qh]])
        V(lambda e, qh=qh: e.tensor_scalar(mu2[:, qh * 512:(qh + 1) * 512], PS[qh][:], 1.0 / D, None,
                                           op0=A.mult), ins=[b_PS[qh]], outs=[b_mu2])
        for dc in range(NDC):
            sq, b_sq = sq3b[dc % 2]
            Sc(lambda e, dc=dc, qh=qh, sq=sq: e.activation(
                sq[:], attn[:, dc * TOK + qh * 512: dc * TOK + qh * 512 + 512], AF.Square),
                ins=[b_at], outs=[b_sq])
            T(lambda e, sq=sq, dc=dc, qh=qh: e.matmul(PS[2 + qh][:], ones_bf[:], sq[:],
                                                      start=(dc == 0), stop=(dc == NDC - 1)),
              ins=[b_sq, b_ones], outs=[b_PS[2 + qh]])
        V(lambda e, qh=qh: e.tensor_scalar(va2[:, qh * 512:(qh + 1) * 512], PS[2 + qh][:], 1.0 / D,
                                           None, op0=A.mult), ins=[b_PS[2 + qh]], outs=[b_va2])
    V(lambda e: e.scalar_tensor_tensor(r2[:], mu2[:], 1.0, mu2[:], op0=A.mult, op1=A.mult),
      ins=[b_mu2], outs=[b_r2])
    V(lambda e: e.tensor_tensor(va2[:], va2[:], r2[:], op=A.subtract), ins=[b_r2, b_va2], outs=[b_va2])
    Sc(lambda e: e.activation(r2[:], va2[:], AF.Ln, bias=epsc[:, 0:1]), ins=[b_va2, b_eps], outs=[b_r2])
    Sc(lambda e: e.activation(r2[:], r2[:], AF.Exp, scale=-0.5), ins=[b_r2], outs=[b_r2])
    for dc in range(NDC):
        xc, b_xc = xcb[dc % 2]
        dma("sync", xc[:], xTo[dc * 128:(dc + 1) * 128, :], outs=[b_xc])
        V(lambda e, dc=dc: e.tensor_tensor(attn[:, dc * TOK:(dc + 1) * TOK],
                                           attn[:, dc * TOK:(dc + 1) * TOK], mu2[:], op=A.subtract),
          ins=[b_mu2], outs=[b_at])
        V(lambda e, dc=dc: e.tensor_tensor(attn[:, dc * TOK:(dc + 1) * TOK],
                                           attn[:, dc * TOK:(dc + 1) * TOK], r2[:], op=A.mult),
          ins=[b_r2], outs=[b_at])
        V(lambda e, dc=dc, xc=xc: e.tensor_tensor(xc[:], xc[:], attn[:, dc * TOK:(dc + 1) * TOK],
                                                  op=A.add), ins=[b_at], outs=[b_xc])
        dma("gpsimd", x1_hbm[dc * 128:(dc + 1) * 128, :], xc[:], ins=[b_xc], outs=[b_x1h])
    p.barrier()
    if stop_after == "p3":
        return nc, p
    ph3.close()
    ph2.close()

    ph4 = ExitStack()
    zar, b_zar = sbp(ph4, "zar", [128, NSL * N], BF16)
    b_zsl = [Buf() for _ in range(NSL)]
    ph4x = ExitStack()
    ln3x, b_l3 = sbp(ph4x, "ln3x", [128, NDC * TOK], BF16)
    zv = zar[:, 0:NSL * N].rearrange("p (s n) -> p s n", n=N)

    ph4s = ExitStack()
    sides[id(ph4s)] = "right"
    mu3, b_mu3 = sbp(ph4s, "mu3", [128, TOK], F32)
    r3, b_r3 = sbp(ph4s, "r3", [128, TOK], F32)
    va3, b_va3 = r3, b_r3
    sq4b = [sbp(ph4s, f"sq4{i}", [128, 512], F32) for i in range(2)]
    x4b = [sbp(ph4s, f"x4{i}", [128, TOK], F32) for i in range(2)]

    for dc in range(NDC):
        xc, b_xc = x4b[dc % 2]
        dma("sync", xc[:], x1_hbm[dc * 128:(dc + 1) * 128, :], ins=[b_x1h], outs=[b_xc])
        for qh in range(2):
            T(lambda e, dc=dc, qh=qh, xc=xc: e.matmul(
                PS[qh][:], ones_bf[:], xc[:, qh * 512:(qh + 1) * 512],
                start=(dc == 0), stop=(dc == NDC - 1)), ins=[b_xc, b_ones], outs=[b_PS[qh]])
            sq, b_sq = sq4b[qh]
            Sc(lambda e, xc=xc, qh=qh, sq=sq: e.activation(sq[:], xc[:, qh * 512:(qh + 1) * 512],
                                                           AF.Square), ins=[b_xc], outs=[b_sq])
            T(lambda e, sq=sq, dc=dc, qh=qh: e.matmul(PS[2 + qh][:], ones_bf[:], sq[:],
                                                      start=(dc == 0), stop=(dc == NDC - 1)),
              ins=[b_sq, b_ones], outs=[b_PS[2 + qh]])
    for qh in range(2):
        V(lambda e, qh=qh: e.tensor_scalar(mu3[:, qh * 512:(qh + 1) * 512], PS[qh][:], 1.0 / D, None,
                                           op0=A.mult), ins=[b_PS[qh]], outs=[b_mu3])
        V(lambda e, qh=qh: e.tensor_scalar(r3[:, qh * 512:(qh + 1) * 512], PS[2 + qh][:], 1.0 / D,
                                           None, op0=A.mult), ins=[b_PS[2 + qh]], outs=[b_r3])
    V(lambda e: e.scalar_tensor_tensor(small[:, 0:1], mu3[:, 0:1], 1.0, mu3[:, 0:1],
                                       op0=A.mult, op1=A.mult), ins=[b_mu3], outs=[b_small]) if False else None
    V(lambda e: e.scalar_tensor_tensor(r3[:], mu3[:], -1.0, mu3[:], op0=A.mult, op1=A.mult),
      ins=[b_mu3], outs=[b_r3]) if False else None
    for qh in range(2):
        sq, b_sq = sq4b[qh]
        V(lambda e, qh=qh, sq=sq: e.scalar_tensor_tensor(
            sq[:], mu3[:, qh * 512:(qh + 1) * 512], 1.0, mu3[:, qh * 512:(qh + 1) * 512],
            op0=A.mult, op1=A.mult), ins=[b_mu3], outs=[b_sq])
        V(lambda e, qh=qh, sq=sq: e.tensor_tensor(r3[:, qh * 512:(qh + 1) * 512],
                                                  r3[:, qh * 512:(qh + 1) * 512], sq[:],
                                                  op=A.subtract), ins=[b_sq], outs=[b_r3])
    Sc(lambda e: e.activation(r3[:], r3[:], AF.Ln, bias=epsc[:, 0:1]), ins=[b_r3, b_eps], outs=[b_r3])
    Sc(lambda e: e.activation(r3[:], r3[:], AF.Exp, scale=-0.5), ins=[b_r3], outs=[b_r3])
    V(lambda e: e.tensor_scalar(r3[:], r3[:], sclv[:, 2:3], None, op0=A.mult),
      ins=[b_sclv], outs=[b_r3])
    for dc in range(NDC):
        xc, b_xc = x4b[dc % 2]
        dma("sync", xc[:], x1_hbm[dc * 128:(dc + 1) * 128, :], ins=[b_x1h], outs=[b_xc])
        V(lambda e, xc=xc: e.tensor_tensor(xc[:], xc[:], mu3[:], op=A.subtract),
          ins=[b_mu3], outs=[b_xc])
        V(lambda e, dc=dc, xc=xc: e.tensor_tensor(ln3x[:, dc * TOK:(dc + 1) * TOK], xc[:], r3[:],
                                                  op=A.mult), ins=[b_xc, b_r3], outs=[b_l3])
    p.barrier()
    ph4s.close()

    ph4w = ExitStack()
    sides[id(ph4w)] = "right"
    tqhb = [sbp(ph4w, f"tqh{i}", [128, 8 * 512], BF16) for i in range(2)]
    QB.update(qwfb=[sbp(ph4w, f"q4wf{i}", [128, 1024], F32) for i in range(2)],
              qab=sbp(ph4w, "q4a", [128, 1024], BF16),
              qtob=[sbp(ph4w, f"q4to{i}", [128, 1024], BF16) for i in range(2)])
    PSL = PS[0:6] + PTb
    b_PSL = b_PS[0:6] + b_PT

    for nb in range(NNB):
        drain_quant_to((nb // 2 + 1) * 16)
        for half in range(2):
            tqh, b_tqh2 = tqhb[(nb * 2 + half) % 2]
            dma("sync", tqh[:].rearrange("p (c q) -> p c q", q=512),
                tqT_hbm[half * 1024:(half + 1) * 1024, nb * 512:(nb + 1) * 512]
                .rearrange("(c p) q -> p c q", p=128),
                ins=[b_tqcol[nb // 2]], outs=[b_tqh2])
            if half == 0:
                dma("gpsimd", bsyn_sl[:], bsyn_h[nb:nb + 1, :], outs=[b_bsyn])
            for sl in range(NSL):
                for d8 in range(8):
                    dc = half * 8 + d8
                    Tq(lambda e, sl=sl, dc=dc, d8=d8, tqh=tqh: e.matmul(
                        PSL[sl][:], ln3x[:, dc * TOK + sl * 128: dc * TOK + sl * 128 + 128],
                        tqh[:, d8 * 512:(d8 + 1) * 512], start=(dc == 0), stop=False),
                        ins=[b_l3, b_tqh2], outs=[b_PSL[sl]])
            drain_quant_jobs(4)
        for sl in range(NSL):
            T(lambda e, sl=sl: e.matmul(PSL[sl][:], ones1_bf[:, 0:128], bsyn_sl[:],
                                        start=False, stop=True), ins=[b_ones1, b_bsyn],
              outs=[b_PSL[sl]])
            Sc(lambda e, sl=sl, nb=nb: e.activation(
                zar[:, sl * N + nb * 512: sl * N + nb * 512 + 512], PSL[sl][:], AF.Identity,
                accum_out=musum[:, sl * NNB + nb: sl * NNB + nb + 1]),
                ins=[b_PSL[sl]], outs=[b_zsl[sl], b_mus])
    drain_quant_jobs(len(quant_jobs))
    p.barrier()
    if stop_after == "p4z":
        return nc, p
    ph4w.close()
    ph4x.close()

    ph4t = ExitStack()
    sides[id(ph4t)] = "right"
    junkz, b_jz = sbp(ph4t, "junkz", [128, N], BF16)
    junkz2, b_jz2 = sbp(ph4t, "junkz2", [128, N], BF16)
    QB.update(qwfb=[sbp(ph4t, f"q5wf{i}", [128, 1024], F32) for i in range(2)],
              qab=sbp(ph4t, "q5a", [128, 1024], BF16),
              qtob=[sbp(ph4t, f"q5to{i}", [128, 1024], BF16) for i in range(2)])
    LO, WID, CHI, TT, CC, GE, GN, DDc, R1, TS, MU, SD = range(12)
    b_cc = [Buf() for _ in range(NSL)]

    def c_(v, sl):
        return stt[:, v * NSL + sl: v * NSL + sl + 1]

    def call(v):
        return stt[:, v * NSL:(v + 1) * NSL]

    for sl in range(NSL):
        V(lambda e, sl=sl: e.tensor_scalar(
            small[:, 0:NNB], musum[:, sl * NNB:(sl + 1) * NNB], 1.0, None, op0=A.mult, op1=A.add,
            accum_out=c_(MU, sl)), ins=[b_mus], outs=[b_small, b_stt])
        Sc(lambda e, sl=sl: e.activation(junkz[:], zar[:, sl * N:(sl + 1) * N], AF.Square,
                                         accum_out=c_(SD, sl)),
           ins=[b_zsl[sl]], outs=[b_jz, b_stt])
    V(lambda e: e.tensor_scalar(call(MU), call(MU), 1.0 / N, None, op0=A.mult),
      ins=[b_stt], outs=[b_stt])
    V(lambda e: e.tensor_scalar(call(SD), call(SD), 1.0 / N, None, op0=A.mult),
      ins=[b_stt], outs=[b_stt])
    V(lambda e: e.scalar_tensor_tensor(call(DDc), call(MU), 1.0, call(MU), op0=A.mult, op1=A.mult),
      ins=[b_stt], outs=[b_stt])
    V(lambda e: e.tensor_tensor(call(SD), call(SD), call(DDc), op=A.subtract),
      ins=[b_stt], outs=[b_stt])
    Sc(lambda e: e.activation(call(SD), call(SD), AF.Ln, bias=epsc[:, 1:2]), ins=[b_stt, b_eps], outs=[b_stt])
    Sc(lambda e: e.activation(call(SD), call(SD), AF.Exp, scale=0.5), ins=[b_stt], outs=[b_stt])
    V(lambda e: e.tensor_scalar(call(WID), call(SD), 0.18, None, op0=A.mult), ins=[b_stt], outs=[b_stt])
    V(lambda e: e.scalar_tensor_tensor(call(LO), call(SD), 1.1516, call(MU), op0=A.mult, op1=A.add),
      ins=[b_stt], outs=[b_stt])
    V(lambda e: e.memset(call(CHI), 0.0), outs=[b_stt])

    def gc(v, g, o=0, w=4):
        return stt[:, v * NSL + 4 * g + o: v * NSL + 4 * g + o + w]

    def biter(g):
        V(lambda e, g=g: e.tensor_scalar(gc(WID, g), gc(WID, g), 0.5, None, op0=A.mult),
          ins=[b_stt], outs=[b_stt])
        V(lambda e, g=g: e.tensor_tensor(gc(TT, g), gc(LO, g), gc(WID, g), op=A.add),
          ins=[b_stt], outs=[b_stt])
        V(lambda e, g=g: e.tensor_scalar(gc(MU, g, 2, 2), gc(TT, g, 2, 2), -1.0, None,
                                         op0=A.mult), ins=[b_stt], outs=[b_stt])
        for sl in (4 * g, 4 * g + 1):
            V(lambda e, sl=sl: e.tensor_scalar(
                junkz[:], zar[:, sl * N:(sl + 1) * N], c_(TT, sl), None, op0=A.is_ge, op1=A.add,
                accum_out=c_(CC, sl)), ins=[b_zsl[sl], b_stt], outs=[b_jz, b_cc[sl]])
        for sl in (4 * g + 2, 4 * g + 3):
            Sc(lambda e, sl=sl: e.activation(
                junkz2[:], zar[:, sl * N:(sl + 1) * N], AF.Sign, bias=c_(MU, sl),
                accum_out=c_(CC, sl)), ins=[b_zsl[sl], b_stt], outs=[b_jz2, b_cc[sl]])
        V(lambda e, g=g: e.tensor_scalar(
            gc(CC, g, 2, 2), gc(CC, g, 2, 2), 0.5, float(N) / 2, op0=A.mult, op1=A.add),
            ins=[b_cc[4 * g + 2], b_cc[4 * g + 3]], outs=[b_cc[4 * g + 2], b_cc[4 * g + 3]])
        V(lambda e, g=g: e.tensor_scalar(gc(GE, g), gc(CC, g), float(K_TOP), None, op0=A.is_ge),
          ins=[b_cc[4 * g + i] for i in range(4)], outs=[b_stt])
        V(lambda e, g=g: e.tensor_scalar(gc(GN, g), gc(GE, g), -1.0, 1.0, op0=A.mult, op1=A.add),
          ins=[b_stt], outs=[b_stt])
        V(lambda e, g=g: e.tensor_tensor(gc(DDc, g), gc(CC, g), gc(CHI, g), op=A.subtract),
          ins=[b_cc[4 * g + i] for i in range(4)] + [b_stt], outs=[b_stt])
        for sl in range(4 * g, 4 * g + 4):
            V(lambda e, sl=sl: e.scalar_tensor_tensor(
                c_(CHI, sl), c_(DDc, sl), c_(GN, sl), c_(CHI, sl), op0=A.mult, op1=A.add),
                ins=[b_stt], outs=[b_stt])
            V(lambda e, sl=sl: e.scalar_tensor_tensor(
                c_(LO, sl), c_(GE, sl), c_(WID, sl), c_(LO, sl), op0=A.mult, op1=A.add),
                ins=[b_stt], outs=[b_stt])

    def gtail(g):
        V(lambda e, g=g: e.tensor_tensor(gc(TT, g), gc(LO, g), gc(WID, g), op=A.add),
          ins=[b_stt], outs=[b_stt])
        V(lambda e, g=g: e.tensor_scalar(gc(R1, g), gc(CHI, g), -float(K_TOP - 1), -1.0,
                                         op0=A.add, op1=A.mult), ins=[b_stt], outs=[b_stt])
        V(lambda e, g=g: e.tensor_scalar(gc(R1, g), gc(R1, g), 7.0, 0.0, op0=A.min, op1=A.max),
          ins=[b_stt], outs=[b_stt])

    def fin(sl):
        jz, b_jzx = (junkz, b_jz) if sl % 2 == 0 else (junkz2, b_jz2)
        V(lambda e, sl=sl: e.scalar_tensor_tensor(
            jz[:], zar[:, sl * N:(sl + 1) * N], c_(TT, sl), zar[:, sl * N:(sl + 1) * N], op0=A.is_lt, op1=A.mult),
            ins=[b_zsl[sl], b_stt], outs=[b_jzx])
        V(lambda e, sl=sl: e.max(out=m8t[:, sl * 8:(sl + 1) * 8], in_=jz[:]),
          ins=[b_jzx], outs=[b_m8])
        V(lambda e, sl=sl: e.scalar_tensor_tensor(
            small[:, 0:8], iota8[:], c_(R1, sl), m8t[:, sl * 8:(sl + 1) * 8],
            op0=A.is_equal, op1=A.mult, accum_out=c_(TS, sl)),
            ins=[b_iota, b_m8, b_stt], outs=[b_small, b_stt])
        V(lambda e, sl=sl: e.tensor_scalar(
            jz[:], zar[:, sl * N:(sl + 1) * N], c_(TS, sl), None, op0=A.is_ge),
            ins=[b_zsl[sl], b_stt], outs=[b_jzx])
        V(lambda e, sl=sl: e.tensor_scalar(
            zar[:, sl * N:(sl + 1) * N], zar[:, sl * N:(sl + 1) * N], 0.0, None, op0=A.max), ins=[b_zsl[sl]], outs=[b_zsl[sl]])
        V(lambda e, sl=sl: e.tensor_tensor(
            zar[:, sl * N:(sl + 1) * N], zar[:, sl * N:(sl + 1) * N], jz[:], op=A.mult), ins=[b_jzx, b_zsl[sl]],
            outs=[b_zsl[sl]])
        for nc4 in range(0, 64, 4):
            pt_i = (nc4 // 4) % 2
            for k2 in range(4):
                T(lambda e, sl=sl, nc4=nc4, k2=k2, pt_i=pt_i: e.matmul(
                    PTb[pt_i][:, k2 * 128:(k2 + 1) * 128],
                    zar[:, sl * N + (nc4 + k2) * 128: sl * N + (nc4 + k2 + 1) * 128],
                    diag01[:], start=True, stop=True),
                    ins=[b_zsl[sl], b_diag01], outs=[b_PT[pt_i]])
            Sc(lambda e, nc4=nc4, pt_i=pt_i: e.copy(jz[:, nc4 * 128:(nc4 + 4) * 128],
                                                    PTb[pt_i][:]),
               ins=[b_PT[pt_i]], outs=[b_jzx])
        V(lambda e, sl=sl: e.tensor_copy(zar[:, sl * N:(sl + 1) * N], jz[:]), ins=[b_jzx], outs=[b_zsl[sl]])

    for it in range(BITERS):
        biter(0)
        biter(1)
    gtail(0)
    gtail(1)
    for sl in range(NSL):
        fin(sl)
        drain_out_jobs(4)
    p.barrier()
    if stop_after == "p4t":
        return nc, p
    ph4t.close()

    ph5 = ExitStack()
    sides[id(ph5)] = "right"
    toutb = [sbp(ph5, f"tout{i}", [128, 32 * 128], BF16) for i in range(2)]
    ybf = [sbp(ph5, f"yb{i}", [128, 512], F32) for i in range(2)]
    QB.update(qwfb=[sbp(ph5, f"q6wf{i}", [128, 1024], F32) for i in range(2)],
              qab=sbp(ph5, "q6a", [128, 1024], BF16),
              qtob=[sbp(ph5, f"q6to{i}", [128, 1024], BF16) for i in range(2)])

    for blk in range(NDC):
        for hh in range(2):
            toq, b_toq = toutb[hh]
            dma("sync", toq[:].rearrange("p (c q) -> p c q", q=128),
                toutT_hbm[hh * 4096:(hh + 1) * 4096, blk * 128:(blk + 1) * 128]
                .rearrange("(c p) q -> p c q", p=128),
                ins=[b_toutc[blk // 8]], outs=[b_toq])
        for th in range(2):
            sl0 = th * 4
            ps_i = th * 2 + blk % 2
            for ncc in range(64):
                toq, b_toq = toutb[ncc // 32]
                nc2 = ncc % 32
                (T if ncc == 63 else Tq)(lambda e, ncc=ncc, nc2=nc2, sl0=sl0, ps_i=ps_i,
                                         toq=toq: e.matmul(
                    PS[ps_i][:], toq[:, nc2 * 128:(nc2 + 1) * 128],
                    bass_zslice(zv, sl0, ncc), start=(ncc == 0), stop=(ncc == 63)),
                    ins=[b_toq] + [b_zsl[sl0 + i] for i in range(4)], outs=[b_PS[ps_i]])
            yb, b_yb = ybf[th]
            Sc(lambda e, blk=blk, ps_i=ps_i, yb=yb: e.activation(
                yb[:], PS[ps_i][:], AF.Identity, bias=bout_sb[:, blk:blk + 1],
                scale=sclv[:, 5:6]), ins=[b_PS[ps_i], b_bout, b_sclv], outs=[b_yb])
            dma("gpsimd", y_hbm[blk * 128:(blk + 1) * 128, th * 512:(th + 1) * 512], yb[:],
                ins=[b_yb], outs=[b_yh])
        if blk < 8:
            drain_out_jobs(8)
    p.barrier()
    ph5.close()
    ph4.close()

    ph6 = ExitStack()
    sides[id(ph6)] = "right"
    yy, b_yy = sbp(ph6, "yy", [128, NDC * TOK], F32)
    mu4, b_mu4 = sbp(ph6, "mu4", [128, TOK], F32)
    r4, b_r4 = sbp(ph6, "r4", [128, TOK], F32)
    va4, b_va4 = sbp(ph6, "va4", [128, TOK], F32)
    mu5, b_mu5 = sbp(ph6, "mu5", [128, TOK], F32)
    r5, b_r5 = sbp(ph6, "r5", [128, TOK], F32)
    va5, b_va5 = sbp(ph6, "va5", [128, TOK], F32)
    sq6b = [sbp(ph6, f"sq6{i}", [128, 512], F32) for i in range(2)]
    x6b = [sbp(ph6, f"x6{i}", [128, TOK], F32) for i in range(2)]

    def lnstat(src_getter, b_srcs, mu, b_mu_, va, b_va_, r, b_r_, ps0, ps1):
        for dc in range(NDC):
            srcap, b_src = src_getter(dc)
            for qh in range(2):
                T(lambda e, dc=dc, qh=qh, srcap=srcap: e.matmul(
                    PS[ps0 + qh][:], ones_bf[:], srcap[:, qh * 512:(qh + 1) * 512],
                    start=(dc == 0), stop=(dc == NDC - 1)), ins=[b_src, b_ones], outs=[b_PS[ps0 + qh]])
                sq, b_sq = sq6b[qh]
                Sc(lambda e, srcap=srcap, qh=qh, sq=sq: e.activation(
                    sq[:], srcap[:, qh * 512:(qh + 1) * 512], AF.Square), ins=[b_src], outs=[b_sq])
                T(lambda e, sq=sq, dc=dc, qh=qh: e.matmul(PS[ps1 + qh][:], ones_bf[:], sq[:],
                                                   start=(dc == 0), stop=(dc == NDC - 1)),
                  ins=[b_sq, b_ones], outs=[b_PS[ps1 + qh]])
        for qh in range(2):
            V(lambda e, qh=qh: e.tensor_scalar(mu[:, qh * 512:(qh + 1) * 512], PS[ps0 + qh][:], 1.0 / D,
                                               None, op0=A.mult), ins=[b_PS[ps0 + qh]], outs=[b_mu_])
            V(lambda e, qh=qh: e.tensor_scalar(va[:, qh * 512:(qh + 1) * 512], PS[ps1 + qh][:], 1.0 / D,
                                               None, op0=A.mult), ins=[b_PS[ps1 + qh]], outs=[b_va_])
        V(lambda e: e.scalar_tensor_tensor(r[:], mu[:], 1.0, mu[:], op0=A.mult, op1=A.mult),
          ins=[b_mu_], outs=[b_r_])
        V(lambda e: e.tensor_tensor(va[:], va[:], r[:], op=A.subtract), ins=[b_r_, b_va_], outs=[b_va_])
        Sc(lambda e: e.activation(r[:], va[:], AF.Ln, bias=epsc[:, 0:1]), ins=[b_va_, b_eps], outs=[b_r_])
        Sc(lambda e: e.activation(r[:], r[:], AF.Exp, scale=-0.5), ins=[b_r_], outs=[b_r_])

    def getx(dc):
        xc, b_xc = x6b[dc % 2]
        dma("sync", xc[:], x1_hbm[dc * 128:(dc + 1) * 128, :], ins=[b_x1h], outs=[b_xc])
        return xc, b_xc
    lnstat(getx, None, mu5, b_mu5, va5, b_va5, r5, b_r5, 0, 2)

    for dc in range(NDC):
        xc, b_xc = x6b[dc % 2]
        dma("sync", xc[:], x1_hbm[dc * 128:(dc + 1) * 128, :], ins=[b_x1h], outs=[b_xc])
        V(lambda e, xc=xc: e.tensor_tensor(xc[:], xc[:], mu5[:], op=A.subtract),
          ins=[b_mu5], outs=[b_xc])
        V(lambda e, xc=xc: e.tensor_tensor(xc[:], xc[:], r5[:], op=A.mult), ins=[b_r5], outs=[b_xc])
        dma("sync", yy[:, dc * TOK:(dc + 1) * TOK], y_hbm[dc * 128:(dc + 1) * 128, :],
            ins=[b_yh], outs=[b_yy])
        V(lambda e, dc=dc, xc=xc: e.tensor_tensor(yy[:, dc * TOK:(dc + 1) * TOK],
                                                  yy[:, dc * TOK:(dc + 1) * TOK], xc[:], op=A.add),
          ins=[b_xc], outs=[b_yy])

    def getyy(dc):
        return yy[:, dc * TOK:(dc + 1) * TOK], b_yy
    lnstat(getyy, None, mu4, b_mu4, va4, b_va4, r4, b_r4, 0, 2)
    for dc in range(NDC):
        V(lambda e, dc=dc: e.tensor_tensor(yy[:, dc * TOK:(dc + 1) * TOK],
                                           yy[:, dc * TOK:(dc + 1) * TOK], mu4[:], op=A.subtract),
          ins=[b_mu4], outs=[b_yy])
        V(lambda e, dc=dc: e.tensor_tensor(yy[:, dc * TOK:(dc + 1) * TOK],
                                           yy[:, dc * TOK:(dc + 1) * TOK], r4[:], op=A.mult),
          ins=[b_r4], outs=[b_yy])
        V(lambda e, dc=dc: e.tensor_scalar(yy[:, dc * TOK:(dc + 1) * TOK],
                                           yy[:, dc * TOK:(dc + 1) * TOK],
                                           gam_sb[:, dc:dc + 1], bet_sb[:, dc:dc + 1],
                                           op0=A.mult, op1=A.add), ins=[b_gam, b_bet], outs=[b_yy])
    lnstat(getyy, None, mu5, b_mu5, va5, b_va5, r5, b_r5, 0, 2)
    for dc in range(NDC):
        xc, b_xc = x6b[dc % 2]
        dma("sync", xc[:], x1_hbm[dc * 128:(dc + 1) * 128, :], ins=[b_x1h], outs=[b_xc])
        V(lambda e, dc=dc: e.tensor_tensor(yy[:, dc * TOK:(dc + 1) * TOK],
                                           yy[:, dc * TOK:(dc + 1) * TOK], mu5[:], op=A.subtract),
          ins=[b_mu5], outs=[b_yy])
        V(lambda e, dc=dc: e.tensor_tensor(yy[:, dc * TOK:(dc + 1) * TOK],
                                           yy[:, dc * TOK:(dc + 1) * TOK], r5[:], op=A.mult),
          ins=[b_r5], outs=[b_yy])
        V(lambda e, dc=dc, xc=xc: e.tensor_tensor(xc[:], xc[:], yy[:, dc * TOK:(dc + 1) * TOK],
                                                  op=A.add), ins=[b_yy], outs=[b_xc])
        dma("gpsimd", y_out[dc * 128:(dc + 1) * 128, :], xc[:], ins=[b_xc], outs=[b_yo])
    p.barrier()
    ph6.close()
    psS.close()
    stP.close()
    return nc, p


def bass_zslice(zv, sl0, ncc):
    return zv[:, sl0:sl0 + 4, ncc * 128:(ncc + 1) * 128]


_PROG_CACHE = {}


def _get_program():
    if "nc" not in _PROG_CACHE:
        nc, p = build_program()
        p.build()
        _PROG_CACHE["nc"] = nc
    return _PROG_CACHE["nc"]


def _host_inputs(inputs):
    import ml_dtypes
    x = np.asarray(inputs["L0_attn_hidden_states"])
    wq = np.asarray(inputs["L0_attn_Wq"])
    wk = np.asarray(inputs["L0_attn_Wk"])
    wv = np.asarray(inputs["L0_attn_Wv"])
    wo = np.asarray(inputs["L0_attn_Wo"])
    bo = np.asarray(inputs["L0_attn_bo"])
    wsyn = np.asarray(inputs["L0_ffn_W_syn"])
    bsyn = np.asarray(inputs["L0_ffn_b_syn"])
    wout = np.asarray(inputs["L0_ffn_W_out"])
    bout = np.asarray(inputs["L0_ffn_b_out"])
    gam = np.asarray(inputs["L0_ffn_gamma"])
    bet = np.asarray(inputs["L0_ffn_beta"])

    shared = {
        "wqT": np.ascontiguousarray(wq.T),
        "wkT": np.ascontiguousarray(wk.T),
        "wvT": np.ascontiguousarray(wv.T),
        "woT": np.ascontiguousarray(wo.T),
        "wsynT": np.ascontiguousarray(wsyn.T),
        "woutT": np.ascontiguousarray(wout.T),
        "bo": np.ascontiguousarray(bo.reshape(NDC, 128)),
        "bout": np.ascontiguousarray(bout.reshape(NDC, 128)),
        "gam": np.ascontiguousarray(gam.reshape(NDC, 128)),
        "bet": np.ascontiguousarray(bet.reshape(NDC, 128)),
        "bsyn": np.ascontiguousarray(bsyn.reshape(NNB, 512)),
        "diag01": np.eye(128, dtype=ml_dtypes.bfloat16),
    }
    in_maps = []
    perms = []
    for c in range(8):
        b, half = c // 2, c % 2
        ch = CHUNKS[half]
        perm = np.concatenate([np.arange(cc * 128, cc * 128 + 128) for cc in ch])
        perms.append(perm)
        xb = x[b].T
        cm = np.zeros((8, 128, 256), np.float32)
        for qt in range(8):
            cq = ch[qt]
            keys = 256 * qt + np.arange(256)[None, :]
            qpos = cq * 128 + np.arange(128)[:, None]
            cm[qt] = np.where(keys <= qpos, 0.0, NEG)
        m = dict(shared)
        m["xTs"] = np.ascontiguousarray(xb)
        m["xTo"] = np.ascontiguousarray(xb[:, perm])
        m["cmask"] = cm
        in_maps.append(m)
    return in_maps, perms


def kernel(**inputs):
    nc = _get_program()
    in_maps, perms = _host_inputs(inputs)
    res = run_bass_kernel_spmd(nc, in_maps, core_ids=list(range(8)))
    x = np.asarray(inputs["L0_attn_hidden_states"])
    out = np.empty((4, S, D), np.float32)
    for c in range(8):
        y = np.asarray(res.results[c]["y"])
        out[c // 2, perms[c], :] = y.T
    return out
